# version 01 (final)
# baseline (speedup 1.0000x reference)
import numpy as np
import concourse.bass as bass
import concourse.mybir as mybir
from concourse.bass_utils import run_bass_kernel_spmd

F32 = mybir.dt.float32
BF16 = mybir.dt.bfloat16
AF = mybir.ActivationFunctionType
ALU = mybir.AluOpType
AX = mybir.AxisListType

T = 2048
D = 1024
NCH = D // 128

STREAMS = {
    "pe":  ("tensor", 1),
    "dve": ("vector", 1),
    "act": ("scalar", 1),
    "pool": ("gpsimd", 1),
    "cc":  ("gpsimd", 1),
    "qs":  ("sync", 16),
    "qg":  ("gpsimd", 16),
    "qa":  ("scalar", 16),
}
ENGINE_OF = {"pe": "tensor", "dve": "vector", "act": "scalar",
             "pool": "gpsimd", "cc": "gpsimd", "qs": "sync",
             "qg": "gpsimd", "qa": "scalar"}
ENGINES = ["tensor", "vector", "scalar", "gpsimd", "sync"]


class Unit:
    __slots__ = ("stream", "emit", "waits", "inc", "cum", "tag")

    def __init__(self, stream, emit, waits, inc, cum, tag):
        self.stream = stream
        self.emit = emit
        self.waits = waits
        self.inc = inc
        self.cum = cum
        self.tag = tag


class Prog:

    def __init__(self, nc):
        self.nc = nc
        self.units = []
        self.cum = {s: 0 for s in STREAMS}
        self.clock = {s: {} for s in STREAMS}
        self.reg = {}
        self.sems = {s: nc.alloc_semaphore(f"sem_{s}") for s in STREAMS}
        self._sbuf_used = 0

    def sb(self, name, shape, dtype):
        t = self.nc.alloc_sbuf_tensor(name, list(shape), dtype)
        return t

    def ps(self, name, shape=(128, 512), dtype=F32):
        return self.nc.alloc_psum_tensor(name, list(shape), dtype)

    @staticmethod
    def _merge(dst, src):
        for k, v in src.items():
            if dst.get(k, -1) < v:
                dst[k] = v

    def unit(self, stream, emit, reads=(), writes=(), inc=None, tag="",
             accumulate=False):
        if inc is None:
            inc = STREAMS[stream][1]
        need = {}
        for k in reads:
            e = self.reg.get(k)
            if e and e["w"]:
                self._merge(need, e["w"])
        for k in writes:
            e = self.reg.get(k)
            if e:
                if e["w"]:
                    if accumulate and stream == "pe":
                        w = {s: v for s, v in e["w"].items() if s != "pe"}
                        self._merge(need, w)
                    else:
                        self._merge(need, e["w"])
                self._merge(need, e["r"])
        if stream in ("qs", "qg", "qa") and self.cum[stream] > 0:
            if need.get(stream, 0) < self.cum[stream]:
                need[stream] = self.cum[stream]
        myclock = self.clock[stream]
        waits = []
        for s2, v in need.items():
            if myclock.get(s2, 0) < v:
                waits.append((s2, v))
        self._merge(myclock, need)
        self.cum[stream] += inc
        cum = self.cum[stream]
        done = dict(myclock)
        done[stream] = cum
        u = Unit(stream, emit, waits, inc, cum, tag)
        self.units.append(u)
        for k in writes:
            old = self.reg.get(k)
            w = dict(done)
            if old and old["w"]:
                self._merge(w, old["w"])
            self.reg[k] = {"w": w, "r": {}}
        for k in reads:
            e = self.reg.setdefault(k, {"w": None, "r": {}})
            self._merge(e["r"], done)
        return u

    def dma(self, queue, out_ap, in_ap, reads=(), writes=(), tag="dma",
            slow=False):
        def emit(eng, o=out_ap, i=in_ap):
            if slow:
                return eng.dma_start(out=o, in_=i,
                                     allow_slow_non_contiguous=True)
            return eng.dma_start(out=o, in_=i)
        return self.unit(queue, emit, reads, writes, inc=16, tag=tag)

    def emit_all(self, block):
        per_engine = {e: [] for e in ENGINES}
        for u in self.units:
            per_engine[ENGINE_OF[u.stream]].append(u)

        def run(eng_name):
            units = per_engine[eng_name]

            def body(eng):
                for u in units:
                    for s2, v in u.waits:
                        eng.wait_ge(self.sems[s2], v)
                    try:
                        last = u.emit(eng)
                    except Exception as ex:
                        raise RuntimeError(f"emit failed in unit {u.tag}: {ex}") from ex
                    if last is not None:
                        last.then_inc(self.sems[u.stream], u.inc)
                    else:
                        assert u.inc == 0, f"unit {u.tag} returned None"
            return body
        block.tensor(run("tensor"))
        block.vector(run("vector"))
        block.scalar(run("scalar"))
        block.gpsimd(run("gpsimd"))
        block.sync(run("sync"))

    def finish_barrier(self):
        finals = {s: c for s, c in self.cum.items() if c > 0}

        def emit(eng):
            for s, v in finals.items():
                if s in ("pool", "qg", "cc"):
                    continue
                eng.wait_ge(self.sems[s], v)
            return None
        self.unit("pool", emit, inc=0, tag="final_barrier")
        def emit2(eng):
            for s in ("qg", "cc"):
                if finals.get(s):
                    eng.wait_ge(self.sems[s], finals[s])
            return None
        self.unit("pool", emit2, inc=0, tag="final_barrier2")



def mm(P, out_ap, pairs, start, stop, reads, writes, tag="mm", accumulate=False,
       tp=None):
    def emit(eng):
        last = None
        n = len(pairs)
        for i, (lh, rh) in enumerate(pairs):
            last = eng.matmul(out_ap, lh, rh,
                              start=(start and i == 0),
                              stop=(stop and i == n - 1),
                              tile_position=tp)
        return last
    return P.unit("pe", emit, reads, writes, tag=tag, accumulate=accumulate)


def mm_multi(P, calls, reads, writes, tag="mm", accumulate=False):
    def emit(eng):
        last = None
        for c in calls:
            (o, lh, rh, st, sp), tp = c[:5], (c[5] if len(c) > 5 else None)
            last = eng.matmul(o, lh, rh, start=st, stop=sp, tile_position=tp)
        return last
    return P.unit("pe", emit, reads, writes, tag=tag, accumulate=accumulate)


def transpose(P, out_ap, in_ap, ident_ap, reads, writes, tag="tr"):
    def emit(eng):
        return eng.transpose(out_ap, in_ap, ident_ap)
    return P.unit("pe", emit, reads, writes, tag=tag)


def act(P, out_ap, in_ap, func, reads, writes, bias=0.0, scale=1.0,
        accum_out=None, tag="act"):
    def emit(eng):
        return eng.activation(out_ap, in_ap, func, bias=bias, scale=scale,
                              accum_out=accum_out)
    return P.unit("act", emit, reads, writes, tag=tag)


def dve(P, fn, reads=(), writes=(), tag="dve"):
    return P.unit("dve", fn, reads, writes, tag=tag)


def pool(P, fn, reads=(), writes=(), tag="pool"):
    return P.unit("pool", fn, reads, writes, tag=tag)



class Env:
    pass


def build_env(P, nc):
    E = Env()
    E.P = P
    E.nc = nc
    E.xn = [P.sb(f"xn{c}", (128, T), BF16) for c in range(NCH)]
    E.wkbig = P.sb("wkbig", (128, 12 * T), BF16)
    E.wk = [E.wkbig[:, i * T:(i + 1) * T] for i in range(12)]
    E.sf = [P.sb("sf0", (128, T), F32), P.sb("sf1", (128, 1024), F32)]
    E.rows = P.sb("rows", (1, 3 * T), F32)
    E.wb = [P.sb(f"wb{i}", (128, 8192), BF16) for i in range(2)]
    E.ones_bf = P.sb("ones_bf", (128, 2), BF16)
    E.ones_f = P.sb("ones_f", (128, 2), F32)
    E.onesr_bf = P.sb("onesr_bf", (1, 128), BF16)
    E.onesr_f = P.sb("onesr_f", (1, 128), F32)
    E.svals = P.sb("svals", (128, 8), F32)
    E.seed = P.sb("seed", (64, 1024), F32)
    E.wbq = P.sb("wbq", (128, 8192), BF16)
    E.scalq = P.sb("scalq", (128, 8), F32)
    E.idn_bf = P.sb("idn_bf", (128, 128), BF16)
    E.tri_bf = P.sb("tri_bf", (128, 128), BF16)
    E.selp = P.sb("selp", (128, 4), F32)
    E.misc = P.sb("misc", (128, 8), F32)
    E.scal = P.sb("scal", (128, 96), F32)
    E.wbB = P.sb("wbB", (128, 128), BF16)
    E.b2p = P.sb("b2p", (8, 1024), F32)
    E.hsbuf = P.sb("hsbuf", (16, T), F32)
    E.idn_f = P.sb("idn_f", (128, 128), F32)
    E.wrf = P.sb("wrf", (128, 64), F32)
    E.half_bf = P.sb("half_bf", (128, 2), BF16)
    E.halfsel_f = P.sb("halfsel_f", (2, 128), F32)
    E.rotb = P.sb("rotb", (128, 2048), BF16)
    E.protT = P.sb("protT", (128, 128), BF16)
    E.mask0 = P.sb("mask0", (128, 1024), BF16)
    E.maskn = P.sb("maskn", (128, 1024), BF16)
    E.psA = P.ps("psA", (128, T), F32)
    E.psB = P.ps("psB", (128, T), F32)
    return E


def pA(E, i):
    return E.psA[:, i * 512:(i + 1) * 512], f"psA{i}"


def pB(E, i):
    return E.psB[:, i * 512:(i + 1) * 512], f"psB{i}"


def row(E, i, n=1):
    return E.rows[0:1, i * T:(i + n) * T]


R_ROWS = "rows"


def load_consts(P, E, nc, ins_aps):
    P.dma("qs", E.ones_bf[:], ins_aps["c_ones_bf"], writes=["ones_bf"])
    P.dma("qs", E.ones_f[:], ins_aps["c_ones_f"], writes=["ones_f"])
    P.dma("qs", E.idn_bf[:], ins_aps["c_idn_bf"], writes=["idn_bf"])
    P.dma("qs", E.tri_bf[:], ins_aps["c_tri_bf"], writes=["tri_bf"])
    P.dma("qs", E.selp[:], ins_aps["c_selp"], writes=["selp"])
    P.dma("qs", E.onesr_bf[:], ins_aps["c_onesr_bf"], writes=["onesr_bf"])
    P.dma("qs", E.onesr_f[:], ins_aps["c_onesr_f"], writes=["onesr_f"])
    P.dma("qs", E.misc[:], ins_aps["c_misc"], writes=["misc"])
    P.dma("qs", E.idn_f[:], ins_aps["c_idn_f"], writes=["idn_f"])
    P.dma("qs", E.half_bf[:], ins_aps["c_half_bf"], writes=["half_bf"])
    P.dma("qs", E.halfsel_f[:], ins_aps["c_halfsel_f"], writes=["halfsel_f"])
    if "c_protT" in ins_aps:
        P.dma("qs", E.protT[:], ins_aps["c_protT"], writes=["protT"])
        P.dma("qs", E.mask0[:], ins_aps["c_mask0"], writes=["mask0"])
        P.dma("qs", E.maskn[:], ins_aps["c_maskn"], writes=["maskn"])



def emit_ln(P, E, x_dram, x_key, out_chs, out_keys, rowbase, gb=None,
            out_dram=None, x_loader=None, tag="ln"):
    for ch in range(NCH):
        if x_loader is not None:
            x_loader(ch, f"{tag}xl{ch}")
        else:
            P.dma("qs", E.sf[0][:], x_dram[ch * 128:(ch + 1) * 128, :],
                  reads=[x_key], writes=["sf0"], tag=f"{tag}xl{ch}")
        xb = E.wk[8]
        dve(P, lambda v, d=xb: v.tensor_copy(d[:], E.sf[0][:]),
            reads=["sf0"], writes=["wk8"], tag=f"{tag}cast{ch}")
        sqc = E.wk[9]
        act(P, sqc[:], E.sf[0][:], AF.Square, reads=["sf0"],
            writes=["wk9"], tag=f"{tag}sq{ch}")
        for tt in range(4):
            a_ap, a_key = pA(E, tt)
            b_ap, b_key = pB(E, tt)
            sl = slice(tt * 512, (tt + 1) * 512)
            mm(P, a_ap[0:1, :], [(E.ones_bf[:, 0:1], xb[:, sl])],
               start=(ch == 0), stop=(ch == NCH - 1), reads=["wk8", "ones_bf"],
               writes=[a_key], tag=f"{tag}sum", accumulate=True)
            mm(P, b_ap[0:1, :], [(E.ones_bf[:, 0:1], sqc[:, sl])],
               start=(ch == 0), stop=(ch == NCH - 1), reads=["wk9", "ones_bf"],
               writes=[b_key], tag=f"{tag}sqsum", accumulate=True)
    rS, rQ, rA = rowbase, rowbase + 1, rowbase + 2
    rB = rA
    dve(P, lambda v: v.tensor_copy(row(E, rS), E.psA[0:1, :]),
        reads=["psA0", "psA1", "psA2", "psA3"], writes=[R_ROWS], tag=f"{tag}evS")
    dve(P, lambda v: v.tensor_copy(row(E, rQ), E.psB[0:1, :]),
        reads=["psB0", "psB1", "psB2", "psB3"], writes=[R_ROWS], tag=f"{tag}evQ")
    dve(P, lambda v: v.tensor_scalar_mul(row(E, rA), row(E, rS), 1.0 / D),
        reads=[R_ROWS], writes=[R_ROWS], tag=f"{tag}mu")
    dve(P, lambda v: v.tensor_scalar_mul(row(E, rQ), row(E, rQ), 1.0 / D),
        reads=[R_ROWS], writes=[R_ROWS], tag=f"{tag}q")
    dve(P, lambda v: v.scalar_tensor_tensor(row(E, rS), row(E, rA), -1.0,
                                            row(E, rA), ALU.mult, ALU.mult),
        reads=[R_ROWS], writes=[R_ROWS], tag=f"{tag}mm")
    dve(P, lambda v: v.tensor_add(row(E, rQ), row(E, rQ), row(E, rS)),
        reads=[R_ROWS], writes=[R_ROWS], tag=f"{tag}var")
    act(P, row(E, rQ), row(E, rQ), AF.Ln, bias=E.misc[0:1, 0:1],
        reads=[R_ROWS, "misc"], writes=[R_ROWS], tag=f"{tag}ln")
    act(P, row(E, rQ), row(E, rQ), AF.Exp, scale=E.misc[0:1, 1:2],
        reads=[R_ROWS, "misc"], writes=[R_ROWS], tag=f"{tag}exp")
    dve(P, lambda v: v.scalar_tensor_tensor(row(E, rB), row(E, rA), -1.0,
                                            row(E, rQ), ALU.mult, ALU.mult),
        reads=[R_ROWS], writes=[R_ROWS], tag=f"{tag}b")
    for tt in range(4):
        a_ap, a_key = pA(E, tt)
        b_ap, b_key = pB(E, tt)
        sl = slice(tt * 512, (tt + 1) * 512)
        mm(P, a_ap, [(E.onesr_f[0:1, :], row(E, rQ)[:, sl])], True, True,
           reads=[R_ROWS, "onesr_f"], writes=[a_key], tag=f"{tag}bcA")
        mm(P, b_ap, [(E.onesr_f[0:1, :], row(E, rB)[:, sl])], True, True,
           reads=[R_ROWS, "onesr_f"], writes=[b_key], tag=f"{tag}bcB")
    for ch in range(NCH):
        if x_loader is not None:
            x_loader(ch, f"{tag}xl2{ch}")
        else:
            P.dma("qs", E.sf[0][:], x_dram[ch * 128:(ch + 1) * 128, :],
                  reads=[x_key], writes=["sf0"], tag=f"{tag}xl2{ch}")
        dve(P, lambda v: v.tensor_mul(E.sf[0][:], E.sf[0][:], E.psA[:]),
            reads=["sf0", "psA0", "psA1", "psA2", "psA3"], writes=["sf0"],
            tag=f"{tag}apA{ch}")
        if gb is None:
            dve(P, lambda v, d=None if out_chs is None else out_chs[ch]:
                v.tensor_add(E.sf[0][:] if d is None else d[:], E.sf[0][:],
                             E.psB[:]),
                reads=["sf0", "psB0", "psB1", "psB2", "psB3"],
                writes=["sf0"] if out_chs is None else [out_keys[ch]],
                tag=f"{tag}apB{ch}")
        else:
            g_aps, b_aps, gkeys = gb
            dve(P, lambda v: v.tensor_add(E.sf[0][:], E.sf[0][:], E.psB[:]),
                reads=["sf0", "psB0", "psB1", "psB2", "psB3"], writes=["sf0"],
                tag=f"{tag}apB{ch}")
            if out_chs is not None:
                dve(P, lambda v, d=out_chs[ch], g=g_aps[ch], b2=b_aps[ch]:
                    v.tensor_scalar(d[:], E.sf[0][:], g, b2, ALU.mult, ALU.add),
                    reads=["sf0"] + list(gkeys), writes=[out_keys[ch]],
                    tag=f"{tag}apG{ch}")
            else:
                dve(P, lambda v, g=g_aps[ch], b2=b_aps[ch]:
                    v.tensor_scalar(E.sf[0][:], E.sf[0][:], g, b2,
                                    ALU.mult, ALU.add),
                    reads=["sf0"] + list(gkeys), writes=["sf0"],
                    tag=f"{tag}apG{ch}")
        if out_dram is not None:
            od, odk = out_dram
            P.dma("qs", od[ch * 128:(ch + 1) * 128, :], E.sf[0][:],
                  reads=["sf0"], writes=[odk], tag=f"{tag}st{ch}")




def elu_evac2(P, E, dst_ap, dst_key, ps_ap, ps_key, bias_ap, bias_key, tag):
    w = ps_ap.shape[1]
    t0 = E.sf[0]
    dve(P, lambda v: v.tensor_scalar(t0[:, 0:w], ps_ap, bias_ap, 0.0,
                                     ALU.add, ALU.min),
        reads=[ps_key, bias_key], writes=["sf0"], tag=tag + "mn")
    t1 = E.sf[1]
    act(P, t1[:, 0:w], t0[:, 0:w], AF.Exp,
        reads=["sf0"], writes=["sf1"], tag=tag + "ex")
    dve(P, lambda v: v.tensor_scalar(t0[:, 0:w], ps_ap, bias_ap, 0.0,
                                     ALU.add, ALU.max),
        reads=[ps_key, bias_key], writes=["sf0"], tag=tag + "rl")
    dve(P, lambda v: v.tensor_add(dst_ap, t0[:, 0:w], t1[:, 0:w]),
        reads=["sf0", "sf1"], writes=[dst_key], tag=tag + "ad")


def emit_l0_attn(P, E, W, dram):
    xnk = [f"xn{c}" for c in range(NCH)]
    P.dma("qg", E.wb[0][:], W["wv_t"], writes=["wb0"], tag="l0a_wv")
    P.dma("qg", E.wb[1][:], W["wk_t"], writes=["wb1"], tag="l0a_wk")
    P.dma("qg", E.wbq[:], W["wq_t"], writes=["wbq"], tag="l0a_wq")
    P.dma("qs", E.scal[:, 0:8], W["bv"], writes=["scal_lo"], tag="l0a_bv")
    P.dma("qs", E.scal[:, 8:16], W["bk"], writes=["scal_hi"], tag="l0a_bk")
    P.dma("qs", E.scalq[:], W["bq"], writes=["scalq"], tag="l0a_bq")
    NCC = 16
    state_sb = E.wk[10]

    def proj_kv(c):
        csl = slice(c * 128, (c + 1) * 128)
        for m in range(8):
            ps_ap, ps_key = pA(E, 0)
            ps_ap = ps_ap[:, 0:128]
            pairs = [(E.wb[1][:, k * 1024 + m * 128: k * 1024 + m * 128 + 128],
                      E.xn[k][:, csl]) for k in range(NCH)]
            mm(P, ps_ap, pairs, True, True, reads=["wb1"] + xnk,
               writes=[ps_key], tag=f"l0a_k{c}_{m}")
            elu_evac2(P, E, E.wk[8][:, m * 128:(m + 1) * 128], "wk8",
                      ps_ap, ps_key, E.scal[:, 8 + m:9 + m], "scal_hi",
                      f"l0ak{c}_{m}")
            ps2, ps2k = pA(E, 1)
            ps2 = ps2[:, 0:128]
            pairs = [(E.wb[0][:, k * 1024 + m * 128: k * 1024 + m * 128 + 128],
                      E.xn[k][:, csl]) for k in range(NCH)]
            mm(P, ps2, pairs, True, True, reads=["wb0"] + xnk,
               writes=[ps2k], tag=f"l0a_v{c}_{m}")
            dve(P, lambda v, m=m, p=ps2: v.tensor_scalar_add(
                E.wk[8][:, 1024 + m * 128:1024 + (m + 1) * 128], p,
                E.scal[:, m:m + 1]),
                reads=[ps2k, "scal_lo"], writes=["wk8"], tag=f"l0av{c}_{m}")
        psT = E.psB[:, 1536:2048].bitcast(BF16)
        for h in range(16):
            pr, hh = divmod(h, 2)
            base = hh * 64
            src_k = E.wk[8][base:base + 64, pr * 128:(pr + 1) * 128]
            src_v = E.wk[8][base:base + 64, 1024 + pr * 128:1024 + (pr + 1) * 128]
            transpose(P, psT[:, (h % 8) * 64:(h % 8) * 64 + 64], src_k,
                      E.idn_bf[base:base + 64, base:base + 64],
                      reads=["wk8", "idn_bf"], writes=["psB3"],
                      tag=f"l0atk{c}_{h}")
            dve(P, lambda v, h=h: v.tensor_copy(
                E.wk[9][:, h * 64:(h + 1) * 64],
                psT[:, (h % 8) * 64:(h % 8) * 64 + 64]),
                reads=["psB3"], writes=["wk9"], tag=f"l0aek{c}_{h}")
            transpose(P, psT[:, (h % 8) * 64:(h % 8) * 64 + 64], src_v,
                      E.idn_bf[base:base + 64, base:base + 64],
                      reads=["wk8", "idn_bf"], writes=["psB3"],
                      tag=f"l0atv{c}_{h}")
            dve(P, lambda v, h=h: v.tensor_copy(
                E.wk[9][:, 1024 + h * 64:1024 + (h + 1) * 64],
                psT[:, (h % 8) * 64:(h % 8) * 64 + 64]),
                reads=["psB3"], writes=["wk9"], tag=f"l0aev{c}_{h}")

    for c in range(NCC):
        proj_kv(c)
        if c == 0:
            tap(P, E, "dbg_kv0", E.wk[8][:], ["wk8"], (128, T), cast=True)
            tap(P, E, "dbg_tm0", E.wk[9][:], ["wk9"], (128, T), cast=True)
        for h in range(16):
            ktm = E.wk[9][:, h * 64:(h + 1) * 64]
            vtm = E.wk[9][:, 1024 + h * 64:1024 + (h + 1) * 64]
            st_ps = E.psA[0:64, 1024 + h * 64: 1024 + (h + 1) * 64]
            st_key = "psA2" if h < 8 else "psA3"
            mm(P, st_ps, [(ktm, vtm)],
               start=(c == 0 and h % 8 == 0), stop=(c == NCC - 1),
               reads=["wk9"], writes=[st_key], tag=f"l0aT{c}_{h}",
               accumulate=not (c == 0 and h % 8 == 0))
    dve(P, lambda v: v.tensor_copy(E.sf[0][0:64, 0:1024], E.psA[0:64, 1024:2048]),
        reads=["psA2", "psA3"], writes=["sf0"], tag="l0a_tot")
    tap(P, E, "dbg_st", E.sf[0][0:64, 0:1024], ["sf0"], (64, 1024))
    P.dma("qg", dram["cc_in"], E.sf[0][0:64, 0:1024], reads=["sf0"],
          writes=["cc_in"], tag="l0a_ccw")
    P.unit("cc", lambda g: g.collective_compute(
        "AllGather", ALU.bypass, replica_groups=[[0, 1], [2, 3], [4, 5], [6, 7]],
        ins=[dram["cc_in"].opt()], outs=[dram["cc_out"].opt()]),
        reads=["cc_in"], writes=["cc_out"], inc=1, tag="l0a_ag")
    P.dma("qs", E.sf[0][0:64, 0:1024], dram["cc_out"][0:64, :],
          reads=["cc_out"], writes=["sf0"], tag="l0a_g0")
    P.dma("qs", E.sf[1][0:64, 0:1024], dram["cc_out"][64:128, :],
          reads=["cc_out"], writes=["sf1"], tag="l0a_g1")
    dve(P, lambda v: v.tensor_scalar_mul(E.sf[0][0:64, 0:1024],
                                         E.sf[0][0:64, 0:1024], E.selp[0:64, 0:1]),
        reads=["sf0", "selp"], writes=["sf0"], tag="l0a_s0")
    dve(P, lambda v: v.scalar_tensor_tensor(
        E.seed[0:64, :], E.sf[1][0:64, 0:1024], E.selp[0:64, 1:2],
        E.sf[0][0:64, 0:1024], ALU.mult, ALU.add),
        reads=["sf0", "sf1", "selp"], writes=["seed"], tag="l0a_seed")
    dve(P, lambda v: v.tensor_copy(state_sb[0:64, 0:1024], E.seed[0:64, :]),
        reads=["seed"], writes=["wk10"], tag="l0a_sc")
    P.dma("qs", state_sb[64:128, 0:1024], state_sb[0:64, 0:1024],
          reads=["wk10"], writes=["wk10"], tag="l0a_seedd")

    for c in range(NCC):
        csl = slice(c * 128, (c + 1) * 128)
        proj_kv(c)
        for m in range(8):
            ps2, ps2k = pA(E, m % 2)
            ps2 = ps2[:, 0:128]
            pairs = [(E.wbq[:, k * 1024 + m * 128: k * 1024 + m * 128 + 128],
                      E.xn[k][:, csl]) for k in range(NCH)]
            mm(P, ps2, pairs, True, True, reads=["wbq"] + xnk,
               writes=[ps2k], tag=f"l0b_q{c}_{m}")
            elu_evac2(P, E, E.wk[11][:, m * 128:(m + 1) * 128], "wk11",
                      ps2, ps2k, E.scalq[:, m:m + 1], "scalq",
                      f"l0bq{c}_{m}")
        for h in range(16):
            pr, hh = divmod(h, 2)
            base = hh * 64
            qc = E.wk[11][base:base + 64, pr * 128:(pr + 1) * 128]
            kc = E.wk[8][base:base + 64, pr * 128:(pr + 1) * 128]
            ktm = E.wk[9][:, h * 64:(h + 1) * 64]
            vtm = E.wk[9][:, 1024 + h * 64:1024 + (h + 1) * 64]
            ps_ap, ps_key = pA(E, h % 2)
            ps_ap = ps_ap[:, 0:128]
            mm(P, ps_ap, [(kc, qc)], True, True,
               reads=["wk8", "wk11"], writes=[ps_key], tag=f"l0bS{c}_{h}")
            pt = E.wk[11][:, 1024 + (h % 4) * 128:1024 + (h % 4 + 1) * 128]
            dve(P, lambda v, d=pt, p=ps_ap: v.tensor_mul(d, p, E.tri_bf[:]),
                reads=[ps_key, "tri_bf"], writes=["wk11"], tag=f"l0bP{c}_{h}")
            ps_ob, ps_ok = pB(E, h % 2)
            ps_o = ps_ob[base:base + 64, 0:128]
            st_l = state_sb[base:base + 64, h * 64:(h + 1) * 64]
            mm_multi(P, [
                (ps_o, vtm, pt, True, False, (0, base)),
                (ps_o, st_l, qc, False, True, (base, base)),
            ], reads=["wk9", "wk11", "wk10"], writes=[ps_ok],
                tag=f"l0bO{c}_{h}")
            dve(P, lambda v, pr=pr, base=base, c=c, po=ps_o:
                v.tensor_copy(E.wk[pr][base:base + 64, c * 128:(c + 1) * 128], po),
                reads=[ps_ok], writes=[f"wk{pr}"], tag=f"l0bOe{c}_{h}")
            st_ps = E.psA[0:64, 1024 + h * 64: 1024 + (h + 1) * 64]
            st_key = "psA2" if h < 8 else "psA3"
            mm(P, st_ps, [(ktm, vtm)],
               start=(c == 0 and h % 8 == 0), stop=(c == NCC - 1),
               reads=["wk9"], writes=[st_key], tag=f"l0bT{c}_{h}",
               accumulate=not (c == 0 and h % 8 == 0))
        dve(P, lambda v: v.tensor_copy(E.sf[0][0:64, 0:1024],
                                       E.psA[0:64, 1024:2048]),
            reads=["psA2", "psA3"], writes=["sf0"], tag=f"l0bStc{c}")
        dve(P, lambda v: v.tensor_add(state_sb[0:64, 0:1024],
                                      E.seed[0:64, :],
                                      E.sf[0][0:64, 0:1024]),
            reads=["sf0", "seed"], writes=["wk10"], tag=f"l0bSta{c}")
        P.dma("qs", state_sb[64:128, 0:1024], state_sb[0:64, 0:1024],
              reads=["wk10"], writes=["wk10"], tag=f"l0bStd{c}")
    if _DEBUG:
        for ch in range(NCH):
            tap(P, E, f"dbg_o{ch}", E.wk[ch], [f"wk{ch}"], (128, T), cast=True)
    P.dma("qg", E.wb[0][:], W["wo_t"], writes=["wb0"], tag="l0c_wo")
    P.dma("qs", E.scal[:, 0:8], W["bo"], writes=["scal_lo"], tag="l0c_bo")
    for tt in range(4):
        sl = slice(tt * 512, (tt + 1) * 512)
        for m in range(8):
            ps_ap, ps_key = (pA(E, m % 4) if m < 4 else pB(E, m % 4))
            ps_ap = ps_ap[:, 0:512]
            pairs = [(E.wb[0][:, k * 1024 + m * 128: k * 1024 + m * 128 + 128],
                      E.wk[k][:, sl]) for k in range(NCH)]
            mm(P, ps_ap, pairs, True, True,
               reads=["wb0"] + [f"wk{k}" for k in range(NCH)],
               writes=[ps_key], tag=f"l0c_{tt}_{m}")
            dve(P, lambda v, m=m, p=ps_ap: v.tensor_scalar_add(
                E.sf[0][:, 0:512], p, E.scal[:, m:m + 1]),
                reads=[ps_key, "scal_lo"], writes=["sf0"], tag=f"l0ce{tt}_{m}")
            P.dma("qs", dram["sspill"][m * 128:(m + 1) * 128, sl],
                  E.sf[0][:, 0:512], reads=["sf0"], writes=["sspill"],
                  tag=f"l0cs{tt}_{m}")


def wslice_ap(dram_t, nk, total_cols, col0, ncols, base=0, row_stride=None):
    rs = row_stride if row_stride is not None else nk * total_cols
    return bass.AP(dram_t, base + col0,
                   [[rs, 128], [total_cols, nk], [1, ncols]])


def emit_l0_ffn_residual(P, E, W, dram):
    xnk = [f"xn{c}" for c in range(NCH)]
    TQ = 256
    NTQ = T // TQ
    sv = E.svals
    for tq in range(NTQ):
        tsl = slice(tq * TQ, (tq + 1) * TQ)
        for mg in range(8):
            dst0 = E.wb[0][:, 0:8 * 512].rearrange("p (k j) -> p k j", k=8)
            P.dma("qg", dst0, wslice_ap(W["k0t"], 8, 4096, mg * 512, 512),
                  writes=["wb0"], tag=f"l0f1w{tq}_{mg}")
            dst1 = E.wb[1][:, 0:8 * 512].rearrange("p (k j) -> p k j", k=8)
            P.dma("qg", dst1, wslice_ap(W["k0bn"], 8, 4096, mg * 512, 512),
                  writes=["wb1"], tag=f"l0f1wb{tq}_{mg}")
            for mm_i in range(4):
                m = mg * 4 + mm_i
                ps_ap, ps_key = pA(E, mm_i % 4)
                ps_ap = ps_ap[:, 0:TQ]
                pairs = []
                for k in range(8):
                    pairs.append((E.wb[0][:, k * 512 + mm_i * 128:
                                          k * 512 + mm_i * 128 + 128],
                                  E.xn[k][:, tsl]))
                for k in range(8):
                    pairs.append((E.wb[1][:, k * 512 + mm_i * 128:
                                          k * 512 + mm_i * 128 + 128],
                                  E.xn[k][:, tsl]))
                mm(P, ps_ap, pairs, True, True,
                   reads=["wb0", "wb1"] + xnk, writes=[ps_key],
                   tag=f"l0f1m{tq}_{m}")
                dst = E.wk[m // 8][:, (m % 8) * TQ:(m % 8 + 1) * TQ]
                act(P, dst, ps_ap, AF.Relu, scale=sv[:, 0:1],
                    reads=[ps_key, "svals"], writes=[f"wk{m // 8}"],
                    tag=f"l0f1e{tq}_{m}")
        if tq == 0:
            for i in range(4):
                tap(P, E, f"dbg_h1_{i}", E.wk[i], [f"wk{i}"], (128, T), cast=True)
        h1k = [f"wk{i}" for i in range(4)]
        for ki in range(32):
            hsrc = E.wk[ki // 8][:, (ki % 8) * TQ:(ki % 8 + 1) * TQ]
            mm(P, E.psB[0:1, 0:TQ], [(E.ones_bf[:, 0:1], hsrc)],
               start=(ki == 0), stop=(ki == 31),
               reads=[f"wk{ki // 8}", "ones_bf"], writes=["psB0"],
               tag=f"l0f_rs{tq}_{ki}", accumulate=(ki > 0))
        dve(P, lambda v: v.tensor_scalar_mul(row(E, 2)[:, 0:TQ],
                                             E.psB[0:1, 0:TQ], -0.5),
            reads=["psB0"], writes=[R_ROWS], tag=f"l0f_rs2{tq}")
        for mg in range(16):
            dstw = E.wb[0][:, 0:32 * 256].rearrange("p (k j) -> p k j", k=32)
            P.dma("qg", dstw, wslice_ap(W["k1"], 32, 4096, mg * 256, 256),
                  writes=["wb0"], tag=f"l0f2w{tq}_{mg}")
            for mm_i in range(2):
                m = mg * 2 + mm_i
                ps_ap, ps_key = pB(E, 1 + mm_i)
                ps_ap = ps_ap[:, 0:TQ]
                pairs = [(E.wb[0][:, k * 256 + mm_i * 128:
                                  k * 256 + mm_i * 128 + 128],
                          E.wk[k // 8][:, (k % 8) * TQ:(k % 8 + 1) * TQ])
                         for k in range(32)]
                mm(P, ps_ap, pairs, True, False,
                   reads=["wb0"] + h1k, writes=[ps_key],
                   tag=f"l0f2m{tq}_{m}")
                mm(P, ps_ap, [(E.onesr_f[0:1, :], row(E, 2)[:, 0:TQ])],
                   False, True, reads=[R_ROWS, "onesr_f"], writes=[ps_key],
                   tag=f"l0f2c{tq}_{m}", accumulate=True)
                dst = E.wk[4 + m // 8][:, (m % 8) * TQ:(m % 8 + 1) * TQ]
                act(P, dst, ps_ap, AF.Relu, scale=sv[:, 1:2],
                    reads=[ps_key, "svals"], writes=[f"wk{4 + m // 8}"],
                    tag=f"l0f2e{tq}_{m}")
        h2k = [f"wk{4 + i}" for i in range(4)]
        for ki in range(32):
            hsrc = E.wk[4 + ki // 8][:, (ki % 8) * TQ:(ki % 8 + 1) * TQ]
            mm(P, E.psB[0:1, 0:TQ], [(E.ones_bf[:, 0:1], hsrc)],
               start=(ki == 0), stop=(ki == 31),
               reads=[f"wk{4 + ki // 8}", "ones_bf"], writes=["psB0"],
               tag=f"l0f_r2{tq}_{ki}", accumulate=(ki > 0))
        dve(P, lambda v: v.tensor_scalar_mul(row(E, 2)[:, 0:TQ],
                                             E.psB[0:1, 0:TQ], -0.5),
            reads=["psB0"], writes=[R_ROWS], tag=f"l0f_r22{tq}")
        for mg in range(4):
            dstw = E.wb[0][:, 0:32 * 256].rearrange("p (k j) -> p k j", k=32)
            P.dma("qg", dstw, wslice_ap(W["k2"], 32, 1024, mg * 256, 256),
                  writes=["wb0"], tag=f"l0f3w{tq}_{mg}")
            for mm_i in range(2):
                m = mg * 2 + mm_i
                ps_ap, ps_key = pB(E, 1 + mm_i)
                ps_ap = ps_ap[:, 0:TQ]
                pairs = [(E.wb[0][:, k * 256 + mm_i * 128:
                                  k * 256 + mm_i * 128 + 128],
                          E.wk[4 + k // 8][:, (k % 8) * TQ:(k % 8 + 1) * TQ])
                         for k in range(32)]
                mm(P, ps_ap, pairs, True, False,
                   reads=["wb0"] + h2k, writes=[ps_key],
                   tag=f"l0f3m{tq}_{m}")
                mm(P, ps_ap, [(E.onesr_f[0:1, :], row(E, 2)[:, 0:TQ])],
                   False, True, reads=[R_ROWS, "onesr_f"], writes=[ps_key],
                   tag=f"l0f3c{tq}_{m}", accumulate=True)
                gsl = E.sf[1][:, 0:TQ]
                act(P, gsl, ps_ap, AF.Silu, scale=sv[:, 2:3],
                    reads=[ps_key, "svals"], writes=["sf1"],
                    tag=f"l0f3g{tq}_{m}")
                ssl = E.sf[1][:, 256:256 + TQ]
                P.dma("qs", ssl, dram["sspill"][m * 128:(m + 1) * 128, tsl],
                      reads=["sspill"], writes=["sf1"], tag=f"l0f_sl{tq}_{m}")
                xsl = E.sf[0][:, 0:TQ]
                P.dma("qs", xsl, dram["xio"][m * 128:(m + 1) * 128, tsl],
                      reads=[dram["xio_key"]], writes=["sf0"], tag=f"l0f_xl{tq}_{m}")
                if tq == 0 and m == 0:
                    tap(P, E, "dbg_g00", gsl, ["sf1"], (128, 256))
                dve(P, lambda v, g=gsl, s_=ssl: v.tensor_mul(g, s_, g),
                    reads=["sf1"], writes=["sf1"], tag=f"l0f_sg{tq}_{m}")
                dve(P, lambda v, g=gsl, x_=xsl: v.tensor_add(x_, x_, g),
                    reads=["sf0", "sf1"], writes=["sf0"], tag=f"l0f_xa{tq}_{m}")
                P.dma("qs", dram["xio2"][m * 128:(m + 1) * 128, tsl], xsl,
                      reads=["sf0"], writes=[dram["xio2_key"]], tag=f"l0f_xs{tq}_{m}")


def emit_l0_preproc(P, E, W, dram):
    for i in range(4):
        sl = slice(i * 8192, (i + 1) * 8192)
        P.dma("qg", E.wb[0][:], W["k0b_flat"][:, sl], writes=["wb0"],
              tag=f"l0pk{i}")
        dve(P, lambda v: v.tensor_scalar_mul(E.wb[0][:], E.wb[0][:], -1.0),
            reads=["wb0"], writes=["wb0"], tag=f"l0pn{i}")
        P.dma("qg", dram["k0bn"][:, sl], E.wb[0][:], reads=["wb0"],
              writes=["k0bn_d"], tag=f"l0ps{i}")
    P.dma("qs", E.svals[:], W["sraw"], writes=["svals"], tag="l0p_sv")
    dve(P, lambda v: v.tensor_scalar_mul(E.svals[:, 0:2], E.svals[:, 0:2], 4.0),
        reads=["svals"], writes=["svals"], tag="l0p_s4")
    dve(P, lambda v: v.tensor_scalar_mul(E.svals[:, 2:3], E.svals[:, 2:3], 2.0),
        reads=["svals"], writes=["svals"], tag="l0p_s2")



_LAYERS = [0, 1, 2, 3]
_DEBUG = False
_SKIP_BIAS = False
_DBG_TENSORS = {}


def tap(P, E, name, src_ap, reads, shape, cast=False):
    if not _DEBUG:
        return
    t = P.nc.dram_tensor(name, list(shape), F32, kind="ExternalOutput")
    _DBG_TENSORS[name] = t

    def em(eng):
        return eng.dma_start(out=t.ap(), in_=src_ap)
    P.unit("qg" if cast else "qs", em, reads=reads, inc=16, tag=name)

U8 = mybir.dt.uint8


def _din(nc, name, shape, dtype=F32):
    return nc.dram_tensor(name, list(shape), dtype, kind="ExternalInput")


def build_program(layers):
    nc = bass.Bass("TRN2", target_bir_lowering=False)
    ins = {}

    def din(name, shape, dtype=F32):
        h = _din(nc, name, shape, dtype)
        ins[name] = h
        return h
    xin = din("xin", (D, T))
    xout = nc.dram_tensor("xout", [D, T], F32, kind="ExternalOutput")
    din("c_ones_bf", (128, 2), BF16)
    din("c_ones_f", (128, 2))
    din("c_onesr_bf", (1, 128), BF16)
    din("c_onesr_f", (1, 128))
    din("c_idn_bf", (128, 128), BF16)
    din("c_tri_bf", (128, 128), BF16)
    din("c_selp", (128, 4))
    din("c_misc", (128, 8))
    din("c_idn_f", (128, 128))
    din("c_half_bf", (128, 2), BF16)
    din("c_halfsel_f", (2, 128))
    if 0 in layers:
        din("l0_wq_t", (128, 8 * 1024))
        din("l0_wk_t", (128, 8 * 1024))
        din("l0_wv_t", (128, 8 * 1024))
        din("l0_wo_t", (128, 8 * 1024))
        din("l0_bq", (128, 8))
        din("l0_bk", (128, 8))
        din("l0_bv", (128, 8))
        din("l0_bo", (128, 8))
        din("l0_k0t", (128, 8 * 4096), U8)
        din("l0_k0b", (128, 8 * 4096), U8)
        din("l0_k1", (128, 32 * 4096), U8)
        din("l0_k2", (128, 32 * 1024), U8)
        din("l0_sraw", (128, 8))
    if 1 in layers:
        din("c_protT", (128, 128), BF16)
        din("c_mask0", (128, 1024), BF16)
        din("c_maskn", (128, 1024), BF16)
        din("c_rotq", (128, 4096), BF16)
        din("c_rotk", (128, 4096), BF16)
        din("c_rotpq", (128, 512), BF16)
        din("c_rotpk", (128, 512), BF16)
        din("l1f_wqkv", (128, 6 * 8 * 3072))
        din("l1f_wo", (128, 6 * 8 * 1024))
        din("l1f_w1p", (128, 6 * 8 * 5632))
        din("l1f_w2p", (128, 6 * 22 * 1024))
        din("l1f_qs", (8, 128))
        din("l1f_ks", (8, 128))
        din("l1f_lng", (128, 48))
        din("l1f_lnb", (128, 48))
        din("l1f_b1p", (128, 264))
        din("l1f_b2", (128, 48))
        din("l1_wq_t", (128, 8192))
        din("l1_wk_t", (128, 8192))
        din("l1_wv_t", (128, 8192))
        din("l1_wo_t", (128, 8192))
        din("l1_bo", (128, 8))
        din("l1_scale_p", (128, 8))
    if 2 in layers:
        din("l2_wq_t", (128, 8192))
        din("l2_wk_t", (128, 8192))
        din("l2_wv_t", (128, 8192))
        din("l2_relpad", (8192, 16))
        din("l2_gamma", (128, 8))
        din("l2_beta", (128, 8))
        din("l2_wr_f", (128, 64))
        din("l2_br_row", (1, 8))
        din("l2_we_t", (128, 8 * 8192))
        din("l2_bep", (8, 1024))
    if 3 in layers:
        din("l3_wd_t", (128, 8192))
        din("l3_bd", (128, 8))
        din("l3_dvec", (128, 8))
        din("l3_bmat_t", (128, 128))
        din("l3_cmat_t", (16, 1024))
        din("l3_alog", (16, 1))
        din("l3_gw_t", (128, 64))
        din("l3_gb", (8, 1))
        din("l3_b1p", (128, 256))
        din("l3_b2p", (8, 1024))
        din("l3_w1", (128, 8 * 8 * 4096))
        din("l3_w2", (128, 8 * 32 * 1024))
    xio_t = nc.dram_tensor("xio", [D, T], F32)
    xio2_t = nc.dram_tensor("xio2", [D, T], F32)
    osp_t = nc.dram_tensor("sspill", [D, T], F32)
    ccin_t = nc.dram_tensor("ccbin", [64, 1024], F32)
    ccout_t = nc.dram_tensor("ccbout", [128, 1024], F32)
    k0bn_t = nc.dram_tensor("k0bn", [128, 8 * 4096], BF16)
    w1bf_t = nc.dram_tensor("w1bf", [128, 8 * 8 * 4096], BF16)
    w2bf_t = nc.dram_tensor("w2bf", [128, 8 * 32 * 1024], BF16)
    ins["w1bf"] = w1bf_t
    ins["w2bf"] = w2bf_t
    ins["relbf"] = nc.dram_tensor("relbf", [131072], BF16)
    qsp_t = nc.dram_tensor("qspill", [D, T], BF16)
    ksp_t = nc.dram_tensor("kspill", [D, 2 * T], BF16)
    vsp_t = nc.dram_tensor("vspill", [D, 2 * T], BF16)
    msk_t = nc.dram_tensor("maskspill", [8, T], F32)
    osp2_t = nc.dram_tensor("osp", [D, T], BF16)
    ix1_t = nc.dram_tensor("ix1", [D, T], F32)
    ix2_t = nc.dram_tensor("ix2", [D, T], F32)
    cc3i_t = nc.dram_tensor("cc3i", [D, 257], F32)
    cc3o_t = nc.dram_tensor("cc3o", [2 * D, 257], F32)
    cc2i_t = nc.dram_tensor("cc2i", [D, T], BF16)
    cc2o_t = nc.dram_tensor("cc2o", [2 * D, T], BF16)

    P = Prog(nc)
    E = build_env(P, nc)
    ins_aps = {k: v.ap() for k, v in ins.items()}
    load_consts(P, E, nc, ins_aps)
    dram = {
        "xio": xio_t.ap(), "xio2": xio2_t.ap(), "sspill": osp_t.ap(),
        "cc_in": ccin_t.ap(), "cc_out": ccout_t.ap(),
        "k0bn": k0bn_t.ap(),
        "qspill": qsp_t.ap(), "kspill": ksp_t.ap(), "vspill": vsp_t.ap(),
        "cc2_in": cc2i_t.ap(), "cc2_out": cc2o_t.ap(),
        "maskspill": msk_t.ap(), "osp": osp2_t.ap(),
        "ix1": ix1_t.ap(), "ix2": ix2_t.ap(),
        "cc3_in": cc3i_t.ap(), "cc3_out": cc3o_t.ap(),
        "xio_key": "xioA", "xio2_key": "xioB",
    }
    P.dma("qs", dram["xio"], ins_aps["xin"], writes=[dram["xio_key"]], tag="xin")

    for li in layers:
        if li == 0:
            emit_layer0(P, E, ins, ins_aps, dram, k0bn_t)
        elif li == 1:
            emit_layer1(P, E, ins, ins_aps, dram)
        elif li == 2:
            emit_layer2(P, E, ins, ins_aps, dram)
        elif li == 3:
            emit_layer3(P, E, ins, ins_aps, dram)
        dram["xio"], dram["xio2"] = dram["xio2"], dram["xio"]
        dram["xio_key"], dram["xio2_key"] = dram["xio2_key"], dram["xio_key"]

    P.dma("qs", xout.ap(), dram["xio"], reads=[dram["xio_key"]], tag="xout")
    if _DEBUG:
        d_xn = nc.dram_tensor("dbg_xn", [D, T], F32, kind="ExternalOutput")
        for ch in range(NCH):
            def em(eng, ch=ch, d=d_xn):
                return eng.dma_start(out=d.ap()[ch * 128:(ch + 1) * 128, :],
                                     in_=E.xn[ch][:])
            P.unit("qg", em, reads=[f"xn{ch}"], inc=16, tag=f"dbgxn{ch}")
        d_s = nc.dram_tensor("dbg_s", [D, T], F32, kind="ExternalOutput")
        P.dma("qs", d_s.ap(), dram["sspill"], reads=["sspill"], tag="dbg_s")
        pass
    P.finish_barrier()
    with nc.Block() as blk:
        P.emit_all(blk)
    return nc, ins


def emit_layer0(P, E, ins, ins_aps, dram, k0bn_t):
    W = {
        "wq_t": ins_aps["l0_wq_t"], "wk_t": ins_aps["l0_wk_t"],
        "wv_t": ins_aps["l0_wv_t"], "wo_t": ins_aps["l0_wo_t"],
        "bq": ins_aps["l0_bq"], "bk": ins_aps["l0_bk"],
        "bv": ins_aps["l0_bv"], "bo": ins_aps["l0_bo"],
        "k0t": ins["l0_k0t"], "k0bn": k0bn_t,
        "k1": ins["l0_k1"], "k2": ins["l0_k2"],
        "k0b_flat": ins_aps["l0_k0b"], "sraw": ins_aps["l0_sraw"],
    }
    emit_l0_preproc(P, E, W, dram)
    emit_ln(P, E, dram["xio"], dram["xio_key"], E.xn,
            [f"xn{c}" for c in range(NCH)], rowbase=0, tag="l0ln")
    emit_l0_attn(P, E, W, dram)
    emit_l0_ffn_residual(P, E, W, dram)


def emit_layer1(P, E, ins, ins_aps, dram):
    raise NotImplementedError


def emit_layer2(P, E, ins, ins_aps, dram):
    raise NotImplementedError


def emit_layer3(P, E, ins, ins_aps, dram):
    raise NotImplementedError



def pack_lhsT(a):
    din, dout = a.shape
    nk = din // 128
    return np.ascontiguousarray(
        a.reshape(nk, 128, dout).transpose(1, 0, 2).reshape(128, nk * dout))


def pack_bias(b):
    return np.ascontiguousarray(b.reshape(-1, 128).T)


def rep_scal(vals):
    out = np.zeros((128, 8), np.float32)
    for i, v in enumerate(vals):
        out[:, i] = v
    return out


def host_constants(core_id):
    c = {}
    c["c_ones_bf"] = np.zeros((128, 2), np.float32)
    c["c_ones_bf"][:, 0] = 1.0
    c["c_ones_f"] = c["c_ones_bf"].copy()
    c["c_onesr_bf"] = np.ones((1, 128), np.float32)
    c["c_onesr_f"] = np.ones((1, 128), np.float32)
    c["c_idn_bf"] = np.eye(128, dtype=np.float32)
    c["c_tri_bf"] = np.triu(np.ones((128, 128), np.float32))
    selp = np.zeros((128, 4), np.float32)
    if core_id % 2 == 1:
        selp[:, 0] = 1.0
    c["c_selp"] = selp
    misc = np.zeros((128, 8), np.float32)
    misc[:, 0] = 1e-5; misc[:, 1] = -0.5; misc[:, 2] = 1e-3
    misc[:, 3] = -1.0; misc[:, 4] = 1.0
    c["c_misc"] = misc
    c["c_idn_f"] = np.eye(128, dtype=np.float32)
    hb = np.zeros((128, 2), np.float32)
    hb[0:64, 0] = 1.0; hb[64:128, 1] = 1.0
    import ml_dtypes as _md
    c["c_half_bf"] = hb.astype(_md.bfloat16)
    hs = np.zeros((2, 128), np.float32)
    hs[0, 0:64] = 1.0; hs[1, 64:128] = 1.0
    c["c_halfsel_f"] = hs
    import ml_dtypes
    for k in list(c):
        if k.endswith("_bf"):
            c[k] = c[k].astype(ml_dtypes.bfloat16)
    return c


def host_l0_weights(I):
    w = {}
    w["l0_wq_t"] = pack_lhsT(np.ascontiguousarray(I["L0_attn_Wq"].T))
    w["l0_wk_t"] = pack_lhsT(np.ascontiguousarray(I["L0_attn_Wk"].T))
    w["l0_wv_t"] = pack_lhsT(np.ascontiguousarray(I["L0_attn_Wv"].T))
    w["l0_wo_t"] = pack_lhsT(np.ascontiguousarray(I["L0_attn_Wo"].T))
    w["l0_bq"] = pack_bias(I["L0_attn_bq"])
    w["l0_bk"] = pack_bias(I["L0_attn_bk"])
    w["l0_bv"] = pack_bias(I["L0_attn_bv"])
    w["l0_bo"] = pack_bias(I["L0_attn_bo"])
    k0 = I["L0_ffn_k0"].astype(np.uint8)
    w["l0_k0t"] = pack_lhsT(k0[0:1024])
    w["l0_k0b"] = pack_lhsT(k0[1024:2048])
    w["l0_k1"] = pack_lhsT(I["L0_ffn_k1"].astype(np.uint8))
    w["l0_k2"] = pack_lhsT(I["L0_ffn_k2"].astype(np.uint8))
    w["l0_sraw"] = rep_scal([float(I["L0_ffn_s0"]), float(I["L0_ffn_s1"]),
                             float(I["L0_ffn_s2"])])
    return w


def kernel(**inputs):
    layers = list(_LAYERS)
    nc, ins = build_program(layers)
    x = np.asarray(inputs["L0_attn_x"])
    shared = {}
    if 0 in layers:
        shared.update(host_l0_weights(inputs))
    if 3 in layers:
        shared.update(host_l3_weights(inputs))
    l2w = [host_l2_weights(inputs, par) for par in (0, 1)] if 2 in layers else None
    if 1 in layers:
        shared.update(host_l1_weights(inputs))
        l1fw = [host_l1f_weights(inputs, h) for h in (0, 1)]
    else:
        l1fw = None
    in_maps = []
    for core in range(8):
        b, half = core // 2, core % 2
        m = dict(shared)
        m.update(host_constants(core))
        if l2w is not None:
            m.update(l2w[half])
        if l1fw is not None:
            m.update(l1fw[half])
        xs = np.ascontiguousarray(x[b, half * T:(half + 1) * T, :].T)
        m["xin"] = xs
        in_maps.append(m)
    res = run_bass_kernel_spmd(nc, in_maps, core_ids=list(range(8)))
    out = np.zeros((4, 4096, D), np.float32)
    for core in range(8):
        b, half = core // 2, core % 2
        out[b, half * T:(half + 1) * T, :] = res.results[core]["xout"].T
    return out



def emit_l3_attn(P, E, ins, ins_aps, dram):
    xnk = [f"xn{c}" for c in range(NCH)]
    P.dma("qg", E.wb[0][:], ins_aps["l3_wd_t"], writes=["wb0"], tag="l3_wd")
    P.dma("qs", E.scal[:, 8:16], ins_aps["l3_bd"], writes=["scal_hi"], tag="l3_bd")
    P.dma("qs", E.scal[:, 0:8], ins_aps["l3_dvec"], writes=["scal_lo"], tag="l3_dv")
    P.dma("qg", E.wbB[:], ins_aps["l3_bmat_t"], writes=["wbB"], tag="l3_bm")
    P.dma("qg", E.wb[1][0:16, 0:1024], ins_aps["l3_cmat_t"], writes=["wb1"],
          tag="l3_cm")
    P.dma("qs", E.svals[0:16, 0:1], ins_aps["l3_alog"], writes=["svals"],
          tag="l3_al")
    act(P, E.svals[0:16, 1:2], E.svals[0:16, 0:1], AF.Exp,
        reads=["svals"], writes=["svals"], tag="l3_A0")
    dve(P, lambda v: v.tensor_scalar_mul(E.svals[0:16, 1:2],
                                         E.svals[0:16, 1:2], -1.0),
        reads=["svals"], writes=["svals"], tag="l3_A")
    for m in range(8):
        for tt in range(4):
            sl = slice(tt * 512, (tt + 1) * 512)
            ps_ap, ps_key = pA(E, tt % 2)
            ps_ap = ps_ap[:, 0:512]
            pairs = [(E.wb[0][:, k * 1024 + m * 128: k * 1024 + m * 128 + 128],
                      E.xn[k][:, sl]) for k in range(NCH)]
            mm(P, ps_ap, pairs, True, True, reads=["wb0"] + xnk,
               writes=[ps_key], tag=f"l3d{m}_{tt}")
            dve(P, lambda v, p=ps_ap, m=m: v.tensor_scalar_add(
                E.sf[0][:, 0:512], p, E.scal[:, 8 + m:9 + m]),
                reads=[ps_key, "scal_hi"], writes=["sf0"], tag=f"l3z{m}_{tt}")
            dve(P, lambda v: v.scalar_tensor_tensor(
                E.sf[1][:, 0:512], E.sf[0][:, 0:512], -1.0, E.sf[0][:, 0:512],
                ALU.mult, ALU.max),
                reads=["sf0"], writes=["sf1"], tag=f"l3ab{m}_{tt}")
            act(P, E.sf[1][:, 0:512], E.sf[1][:, 0:512], AF.Exp,
                scale=E.misc[:, 3:4],
                reads=["sf1", "misc"], writes=["sf1"], tag=f"l3ex{m}_{tt}")
            act(P, E.sf[1][:, 0:512], E.sf[1][:, 0:512], AF.Ln,
                bias=E.misc[:, 4:5],
                reads=["sf1", "misc"], writes=["sf1"], tag=f"l3ll{m}_{tt}")
            dve(P, lambda v: v.tensor_single_scalar(
                E.sf[0][:, 0:512], E.sf[0][:, 0:512], 0.0, ALU.max),
                reads=["sf0"], writes=["sf0"], tag=f"l3rl{m}_{tt}")
            dve(P, lambda v: v.tensor_add(E.sf[0][:, 0:512], E.sf[0][:, 0:512],
                                          E.sf[1][:, 0:512]),
                reads=["sf0", "sf1"], writes=["sf0"], tag=f"l3sp{m}_{tt}")
            dve(P, lambda v: v.tensor_copy(E.wk[8][:, 0:512], E.sf[0][:, 0:512]),
                reads=["sf0"], writes=["wk8"], tag=f"l3sc{m}_{tt}")
            b_ap, b_key = pB(E, tt)
            mm(P, b_ap[0:1, :], [(E.ones_bf[:, 0:1], E.wk[8][:, 0:512])],
               start=(m == 0), stop=(m == 7), reads=["wk8", "ones_bf"],
               writes=[b_key], tag=f"l3ds{tt}", accumulate=(m > 0))
    dve(P, lambda v: v.tensor_scalar_mul(row(E, 0), E.psB[0:1, :], 1.0 / D),
        reads=["psB0", "psB1", "psB2", "psB3"], writes=[R_ROWS], tag="l3dm")
    for tt in range(4):
        sl = slice(tt * 512, (tt + 1) * 512)
        b_ap, b_key = pB(E, tt)
        mm(P, b_ap[0:16, :], [(E.wbB[:, k * 16:(k + 1) * 16], E.xn[k][:, sl])
                              for k in range(NCH)], True, True,
           reads=["wbB"] + xnk, writes=[b_key], tag=f"l3bx{tt}")
        a_ap, a_key = pA(E, tt)
        mm(P, a_ap[0:16, :], [(E.onesr_f[0:1, 0:16], row(E, 0)[:, sl])],
           True, True, reads=[R_ROWS, "onesr_f"], writes=[a_key],
           tag=f"l3db{tt}")
    act(P, E.sf[0][0:16, :], E.psA[0:16, :], AF.Exp, scale=E.svals[0:16, 1:2],
        reads=["psA0", "psA1", "psA2", "psA3", "svals"], writes=["sf0"],
        tag="l3dA")
    dve(P, lambda v: v.tensor_copy(E.sf[1][0:16, 0:1024], E.psA[0:16, 0:1024]),
        reads=["psA0", "psA1"], writes=["sf1"], tag="l3dc0")
    dve(P, lambda v: v.tensor_mul(E.psB[0:16, 0:1024], E.psB[0:16, 0:1024],
                                  E.sf[1][0:16, 0:1024]),
        reads=["psB0", "psB1", "sf1"], writes=["psB0", "psB1"], tag="l3dx0")
    dve(P, lambda v: v.tensor_copy(E.sf[1][0:16, 0:1024], E.psA[0:16, 1024:2048]),
        reads=["psA2", "psA3"], writes=["sf1"], tag="l3dc1")
    dve(P, lambda v: v.tensor_mul(E.psB[0:16, 1024:2048],
                                  E.psB[0:16, 1024:2048],
                                  E.sf[1][0:16, 0:1024]),
        reads=["psB2", "psB3", "sf1"], writes=["psB2", "psB3"], tag="l3dx1")
    dve(P, lambda v: v.tensor_tensor_scan(
        E.hsbuf[0:16, 0:1024], E.sf[0][0:16, 0:1024], E.psB[0:16, 0:1024],
        0.0, ALU.mult, ALU.add),
        reads=["sf0", "psB0", "psB1"], writes=["hsbuf"], tag="l3scA0")
    dve(P, lambda v: v.tensor_tensor_scan(
        E.hsbuf[0:16, 1024:2048], E.sf[0][0:16, 1024:2048],
        E.psB[0:16, 1024:2048], E.hsbuf[0:16, 1023:1024],
        ALU.mult, ALU.add),
        reads=["sf0", "psB2", "psB3", "hsbuf"], writes=["hsbuf"], tag="l3scA1")
    dve(P, lambda v: v.tensor_copy(E.sf[1][0:16, 0:1], E.hsbuf[0:16, 2047:2048]),
        reads=["hsbuf"], writes=["sf1"], tag="l3he")
    P.dma("qg", bass.AP(dram["cc_in"].tensor, 0, [[1, 16], [1, 1]]),
          E.sf[1][0:16, 0:1], reads=["sf1"], writes=["cc_in"], tag="l3ccw")
    P.unit("cc", lambda g: g.collective_compute(
        "AllGather", ALU.bypass, replica_groups=[[0, 1], [2, 3], [4, 5], [6, 7]],
        ins=[dram["cc_in"].opt()], outs=[dram["cc_out"].opt()]),
        reads=["cc_in"], writes=["cc_out"], inc=1, tag="l3ag")
    P.dma("qs", E.sf[1][0:16, 0:1],
          bass.AP(dram["cc_out"].tensor, 0, [[1, 16], [1, 1]]),
          reads=["cc_out"], writes=["sf1"], tag="l3g0")
    P.dma("qs", E.sf[1][0:16, 1:2],
          bass.AP(dram["cc_out"].tensor, 64 * 1024, [[1, 16], [1, 1]]),
          reads=["cc_out"], writes=["sf1"], tag="l3g1")
    dve(P, lambda v: v.tensor_scalar_mul(E.sf[1][0:16, 0:1],
                                         E.sf[1][0:16, 0:1], E.selp[0:16, 0:1]),
        reads=["sf1", "selp"], writes=["sf1"], tag="l3s0")
    dve(P, lambda v: v.scalar_tensor_tensor(
        E.sf[1][0:16, 2:3], E.sf[1][0:16, 1:2], E.selp[0:16, 1:2],
        E.sf[1][0:16, 0:1], ALU.mult, ALU.add),
        reads=["sf1", "selp"], writes=["sf1"], tag="l3seed")
    dve(P, lambda v: v.tensor_tensor_scan(
        E.hsbuf[0:16, 0:1024], E.sf[0][0:16, 0:1024], E.psB[0:16, 0:1024],
        E.sf[1][0:16, 2:3], ALU.mult, ALU.add),
        reads=["sf0", "psB0", "psB1", "sf1"], writes=["hsbuf"], tag="l3scB0")
    dve(P, lambda v: v.tensor_tensor_scan(
        E.hsbuf[0:16, 1024:2048], E.sf[0][0:16, 1024:2048],
        E.psB[0:16, 1024:2048], E.hsbuf[0:16, 1023:1024],
        ALU.mult, ALU.add),
        reads=["sf0", "psB2", "psB3", "hsbuf"], writes=["hsbuf"], tag="l3scB1")
    dve(P, lambda v: v.tensor_copy(E.wk[8][0:16, :], E.hsbuf[0:16, :]),
        reads=["hsbuf"], writes=["wk8"], tag="l3hsc")
    for m in range(8):
        for tt in range(4):
            sl = slice(tt * 512, (tt + 1) * 512)
            a_ap, a_key = pA(E, tt % 2)
            a_ap = a_ap[:, 0:512]
            mm(P, a_ap, [(E.wb[1][0:16, m * 128:(m + 1) * 128],
                          E.wk[8][0:16, sl])], True, True,
               reads=["wb1", "wk8"], writes=[a_key], tag=f"l3y{m}_{tt}")
            dve(P, lambda v, m=m, sl=sl: v.tensor_scalar_mul(
                E.sf[0][:, 0:512], E.xn[m][:, sl], E.scal[:, m:m + 1]),
                reads=[f"xn{m}", "scal_lo"], writes=["sf0"], tag=f"l3dxn{m}_{tt}")
            dve(P, lambda v, p=a_ap, k2=a_key: v.tensor_add(
                E.sf[0][:, 0:512], E.sf[0][:, 0:512], p),
                reads=["sf0", a_key], writes=["sf0"], tag=f"l3ya{m}_{tt}")
            P.dma("qs", dram["sspill"][m * 128:(m + 1) * 128, sl],
                  E.sf[0][:, 0:512], reads=["sf0"], writes=["sspill"],
                  tag=f"l3ys{m}_{tt}")


def emit_l3_ffn_residual(P, E, ins, ins_aps, dram):
    xnk = [f"xn{c}" for c in range(NCH)]
    P.dma("qg", E.wbB[:, 0:64], ins_aps["l3_gw_t"], writes=["wbB"], tag="l3gw")
    P.dma("qs", E.svals[0:8, 2:3], ins_aps["l3_gb"], writes=["svals"], tag="l3gb")
    for tt in range(4):
        sl = slice(tt * 512, (tt + 1) * 512)
        a_ap, a_key = pA(E, tt)
        mm(P, a_ap[0:8, :], [(E.wbB[:, k * 8:(k + 1) * 8], E.xn[k][:, sl])
                             for k in range(NCH)], True, True,
           reads=["wbB"] + xnk, writes=[a_key], tag=f"l3gl{tt}")
    dve(P, lambda v: v.tensor_scalar_add(E.sf[0][0:8, :], E.psA[0:8, :],
                                         E.svals[0:8, 2:3]),
        reads=["psA0", "psA1", "psA2", "psA3", "svals"], writes=["sf0"],
        tag="l3glb")
    for ti in range(16):
        csl = slice(ti * 128, (ti + 1) * 128)
        transpose(P, E.psB[0:128, 0:8], E.sf[0][0:8, csl], E.idn_f[0:8, 0:8],
                  reads=["sf0", "idn_f"], writes=["psB0"], tag=f"l3tt{ti}")
        lt = E.sf[1][:, 0:8]
        m1 = E.sf[1][:, 8:9]
        is1 = E.sf[1][:, 16:24]
        l2 = E.sf[1][:, 24:32]
        m2s = E.sf[1][:, 9:10]
        is2 = E.sf[1][:, 32:40]
        d12 = E.sf[1][:, 10:11]
        den = E.sf[1][:, 11:12]
        w2c = E.sf[1][:, 12:13]
        dw = E.sf[1][:, 40:48]
        dve(P, lambda v, a=lt: v.tensor_copy(a, E.psB[0:128, 0:8]),
            reads=["psB0"], writes=["sf1"], tag=f"l3te{ti}")
        dve(P, lambda v, a=m1, b=lt: v.tensor_reduce(a, b, AX.X, ALU.max),
            reads=["sf1"], writes=["sf1"], tag=f"l3m1{ti}")
        dve(P, lambda v, a=is1, b=lt, c=m1: v.tensor_single_scalar(a, b, c, ALU.is_ge),
            reads=["sf1"], writes=["sf1"], tag=f"l3i1{ti}")
        dve(P, lambda v, a=l2, b=is1, c=lt: v.scalar_tensor_tensor(
            a, b, -1e30, c, ALU.mult, ALU.add),
            reads=["sf1"], writes=["sf1"], tag=f"l3l2{ti}")
        dve(P, lambda v, a=m2s, b=l2: v.tensor_reduce(a, b, AX.X, ALU.max),
            reads=["sf1"], writes=["sf1"], tag=f"l3m2{ti}")
        dve(P, lambda v, a=is2, b=l2, c=m2s: v.tensor_single_scalar(a, b, c, ALU.is_ge),
            reads=["sf1"], writes=["sf1"], tag=f"l3i2{ti}")
        dve(P, lambda v, a=d12, b=m2s, c=m1: v.tensor_sub(a, b, c),
            reads=["sf1"], writes=["sf1"], tag=f"l3d12{ti}")
        act(P, d12, d12, AF.Exp, reads=["sf1"], writes=["sf1"], tag=f"l3e2{ti}")
        dve(P, lambda v, a=den, b=d12: v.tensor_scalar_add(a, b, 1.0),
            reads=["sf1"], writes=["sf1"], tag=f"l3den{ti}")
        dve(P, lambda v, a=den: v.reciprocal(a, a),
            reads=["sf1"], writes=["sf1"], tag=f"l3rc{ti}")
        dve(P, lambda v, a=w2c, b=d12, c=den: v.tensor_mul(a, b, c),
            reads=["sf1"], writes=["sf1"], tag=f"l3w2{ti}")
        dve(P, lambda v, a=dw, b=is1, c=den: v.tensor_single_scalar(a, b, c, ALU.mult),
            reads=["sf1"], writes=["sf1"], tag=f"l3dw1{ti}")
        dve(P, lambda v, a=dw, b=is2, c=w2c: v.scalar_tensor_tensor(
            a, b, c, a, ALU.mult, ALU.add),
            reads=["sf1"], writes=["sf1"], tag=f"l3dw2{ti}")
        transpose(P, E.psB[0:8, 0:128], dw, E.idn_f[:],
                  reads=["sf1", "idn_f"], writes=["psB0"], tag=f"l3tb{ti}")
        dve(P, lambda v, csl=csl: v.tensor_copy(E.sf[0][0:8, csl],
                                                E.psB[0:8, 0:128]),
            reads=["psB0"], writes=["sf0"], tag=f"l3dwc{ti}")
    P.dma("qs", E.b2p[:], ins_aps["l3_b2p"], writes=["b2p"], tag="l3b2")
    for tb in range(4):
        tsl = slice(tb * 512, (tb + 1) * 512)
        for e in range(8):
            P.dma("qs", row(E, 1)[:, 0:512], E.sf[0][e:e + 1, tsl],
                  reads=["sf0"], writes=[R_ROWS], tag=f"l3dwr{tb}_{e}")
            mm(P, E.psB[:, 512:1024], [(E.onesr_f[0:1, :], row(E, 1)[:, 0:512])],
               True, True, reads=[R_ROWS, "onesr_f"], writes=["psB1"],
               tag=f"l3dwb{tb}_{e}")
            P.dma("qs", E.scal[:, 16:48],
                  bass.AP(ins["l3_b1p"], e * 32, [[256, 128], [1, 32]]),
                  writes=["scal_b1"], tag=f"l3b1{tb}_{e}")
            for m2g in range(8):
                dstw = E.wb[0][:, 0:8 * 512].rearrange("p (k j) -> p k j", k=8)
                P.dma("qs", dstw,
                      wslice_ap(ins["w1bf"], 8, 4096, m2g * 512, 512,
                                base=e * 8 * 4096,
                                row_stride=8 * 8 * 4096),
                      reads=["w1bf"], writes=["wb0"], tag=f"l3w1{tb}_{e}_{m2g}")
                for mi in range(4):
                    m2 = m2g * 4 + mi
                    a_ap, a_key = pA(E, mi % 2)
                    a_ap = a_ap[:, 0:512]
                    pairs = [(E.wb[0][:, k * 512 + mi * 128:
                                      k * 512 + mi * 128 + 128],
                              E.xn[k][:, tsl]) for k in range(NCH)]
                    mm(P, a_ap, pairs, True, True, reads=["wb0"] + xnk,
                       writes=[a_key], tag=f"l3h{tb}_{e}_{m2}")
                    act(P, E.sf[1][:, 512:1024], a_ap, AF.Silu,
                        bias=E.scal[:, 16 + m2:17 + m2],
                        reads=[a_key, "scal_b1"], writes=["sf1"],
                        tag=f"l3si{tb}_{e}_{m2}")
                    dve(P, lambda v, m2=m2: v.tensor_mul(
                        E.wk[m2 // 4][:, (m2 % 4) * 512:(m2 % 4 + 1) * 512],
                        E.sf[1][:, 512:1024], E.psB[:, 512:1024]),
                        reads=["sf1", "psB1"], writes=[f"wk{m2 // 4}"],
                        tag=f"l3hm{tb}_{e}_{m2}")
            for mg in range(4):
                wdst = E.wb[1][:, 0:8192].rearrange("p (k j) -> p k j", k=32)
                P.dma("qs", wdst,
                      wslice_ap(ins["w2bf"], 32, 1024, mg * 256, 256,
                                base=e * 32 * 1024,
                                row_stride=8 * 32 * 1024),
                      reads=["w2bf"], writes=["wb1"], tag=f"l3w2{tb}_{e}_{mg}")
                for mi in range(2):
                    m = mg * 2 + mi
                    ps_y = E.psB[:, 1024 + (m % 2) * 512:1024 + (m % 2 + 1) * 512]
                    ps_yk = "psB2" if m % 2 == 0 else "psB3"
                    pairs = [(E.wb[1][:, k * 256 + mi * 128:
                                      k * 256 + mi * 128 + 128],
                              E.wk[k // 4][:, (k % 4) * 512:(k % 4 + 1) * 512])
                             for k in range(32)]
                    mm(P, ps_y, pairs, True, True,
                       reads=["wb1"] + [f"wk{i}" for i in range(8)],
                       writes=[ps_yk], tag=f"l3ye{tb}_{e}_{m}")
                    yd = E.wk[8 + m // 2][:].bitcast(F32)
                    ysl = yd[:, (m % 2) * 512:(m % 2 + 1) * 512]
                    if e == 0:
                        dve(P, lambda v, o=ysl, p=ps_y: v.tensor_copy(o, p),
                            reads=[ps_yk], writes=[f"wk{8 + m // 2}"],
                            tag=f"l3yc{tb}_{e}_{m}")
                    else:
                        dve(P, lambda v, o=ysl, p=ps_y: v.tensor_add(o, o, p),
                            reads=[ps_yk, f"wk{8 + m // 2}"],
                            writes=[f"wk{8 + m // 2}"],
                            tag=f"l3yc{tb}_{e}_{m}")
        for m in range(8):
            ps_y = E.psB[:, 1024:1536]
            mm(P, ps_y, [(E.b2p[:, m * 128:(m + 1) * 128],
                          E.sf[0][0:8, tsl])], True, True,
               reads=["b2p", "sf0"], writes=["psB2"], tag=f"l3bb{tb}_{m}")
            yd = E.wk[8 + m // 2][:].bitcast(F32)
            ysl = yd[:, (m % 2) * 512:(m % 2 + 1) * 512]
            dve(P, lambda v, o=ysl, p=ps_y: v.tensor_add(o, o, p),
                reads=["psB2", f"wk{8 + m // 2}"], writes=[f"wk{8 + m // 2}"],
                tag=f"l3ba{tb}_{m}")
            gsl = E.sf[1][:, 0:512]
            act(P, gsl, ysl, AF.Silu, reads=[f"wk{8 + m // 2}"],
                writes=["sf1"], tag=f"l3gg{tb}_{m}")
            ssl = E.sf[1][:, 512:1024]
            P.dma("qs", ssl, dram["sspill"][m * 128:(m + 1) * 128, tsl],
                  reads=["sspill"], writes=["sf1"], tag=f"l3sl{tb}_{m}")
            xsl = E.wb[0][:].bitcast(F32)[:, 0:512]
            P.dma("qs", xsl, dram["xio"][m * 128:(m + 1) * 128, tsl],
                  reads=[dram["xio_key"]], writes=["wb0"], tag=f"l3xl{tb}_{m}")
            dve(P, lambda v, g=gsl, s_=ssl: v.tensor_mul(g, s_, g),
                reads=["sf1"], writes=["sf1"], tag=f"l3sg{tb}_{m}")
            dve(P, lambda v, g=gsl, x_=xsl: v.tensor_add(x_, x_, g),
                reads=["wb0", "sf1"], writes=["wb0"], tag=f"l3xa{tb}_{m}")
            P.dma("qs", dram["xio2"][m * 128:(m + 1) * 128, tsl], xsl,
                  reads=["wb0"], writes=[dram["xio2_key"]], tag=f"l3xs{tb}_{m}")


def emit_l3_preproc(P, E, ins, ins_aps, dram):
    tot1 = 8 * 8 * 4096
    for i in range(tot1 // 8192):
        sl = slice(i * 8192, (i + 1) * 8192)
        P.dma("qg", E.wb[0][:], ins_aps["l3_w1"][:, sl], writes=["wb0"],
              tag=f"l3p1a{i}")
        P.dma("qg", bass.AP(ins["w1bf"], i * 8192, [[tot1, 128], [1, 8192]]),
              E.wb[0][:], reads=["wb0"], writes=["w1bf"], tag=f"l3p1b{i}")
    tot2 = 8 * 32 * 1024
    for i in range(tot2 // 8192):
        sl = slice(i * 8192, (i + 1) * 8192)
        P.dma("qg", E.wb[0][:], ins_aps["l3_w2"][:, sl], writes=["wb0"],
              tag=f"l3p2a{i}")
        P.dma("qg", bass.AP(ins["w2bf"], i * 8192, [[tot2, 128], [1, 8192]]),
              E.wb[0][:], reads=["wb0"], writes=["w2bf"], tag=f"l3p2b{i}")


def emit_layer3(P, E, ins, ins_aps, dram):
    emit_l3_preproc(P, E, ins, ins_aps, dram)
    emit_ln(P, E, dram["xio"], dram["xio_key"], E.xn,
            [f"xn{c}" for c in range(NCH)], rowbase=0, tag="l3ln")
    emit_l3_attn(P, E, ins, ins_aps, dram)
    emit_l3_ffn_residual(P, E, ins, ins_aps, dram)


def host_l3_weights(I):
    w = {}
    w["l3_wd_t"] = pack_lhsT(np.ascontiguousarray(I["L3_attn_W_delta"].T))
    w["l3_bd"] = pack_bias(I["L3_attn_b_delta"])
    w["l3_dvec"] = pack_bias(I["L3_attn_D"])
    w["l3_bmat_t"] = pack_lhsT(np.ascontiguousarray(I["L3_attn_B"].T))
    bm = np.zeros((128, 128), np.float32)
    bm[:, 0:128] = np.pad(w["l3_bmat_t"], ((0, 0), (0, 0)))
    w["l3_bmat_t"] = w["l3_bmat_t"]
    if w["l3_bmat_t"].shape[1] < 128:
        w["l3_bmat_t"] = np.pad(w["l3_bmat_t"],
                                ((0, 0), (0, 128 - w["l3_bmat_t"].shape[1])))
    w["l3_cmat_t"] = np.ascontiguousarray(I["L3_attn_C"].T)
    w["l3_alog"] = np.ascontiguousarray(I["L3_attn_A_log"].reshape(16, 1))
    gw = pack_lhsT(I["L3_ffn_gate_w"])
    w["l3_gw_t"] = np.pad(gw, ((0, 0), (0, 64 - gw.shape[1])))
    w["l3_gb"] = np.ascontiguousarray(I["L3_ffn_gate_b"].reshape(8, 1))
    w["l3_b1p"] = np.concatenate(
        [pack_bias(I["L3_ffn_b1"][e]) for e in range(8)], axis=1)
    w["l3_b2p"] = np.stack([I["L3_ffn_b2"][e] for e in range(8)])
    w["l3_w1"] = np.concatenate(
        [pack_lhsT(I["L3_ffn_W1"][e]) for e in range(8)], axis=1)
    w["l3_w2"] = np.concatenate(
        [pack_lhsT(I["L3_ffn_W2"][e]) for e in range(8)], axis=1)
    return w



def emit_l2_attn(P, E, ins, ins_aps, dram):
    xnk = [f"xn{c}" for c in range(NCH)]
    for ch in range(NCH):
        P.dma("qg", dram["cc2_in"][ch * 128:(ch + 1) * 128, :], E.xn[ch][:],
              reads=[f"xn{ch}"], writes=["cc2_in"], tag=f"l2agw{ch}")
    P.unit("cc", lambda g: g.collective_compute(
        "AllGather", ALU.bypass, replica_groups=[[0, 1], [2, 3], [4, 5], [6, 7]],
        ins=[dram["cc2_in"].opt()], outs=[dram["cc2_out"].opt()]),
        reads=["cc2_in"], writes=["cc2_out"], inc=1, tag="l2ag")
    rp = ins_aps["l2_relpad"].rearrange("a b -> (a b)").rearrange(
        "(p n) -> p n", p=128)
    P.dma("qg", E.wb[0][:, 0:1024], rp, writes=["wb0"], tag="l2rp1")
    P.dma("qg", bass.AP(ins["relbf"], 0, [[1024, 128], [1, 1024]]),
          E.wb[0][:, 0:1024], reads=["wb0"], writes=["relbf"], tag="l2rp2")
    P.dma("qg", E.wb[0][:], ins_aps["l2_wq_t"], writes=["wb0"], tag="l2wq")
    for m in range(8):
        for tt in range(4):
            sl = slice(tt * 512, (tt + 1) * 512)
            ps_ap, ps_key = pA(E, (m * 4 + tt) % 4)
            ps_ap = ps_ap[:, 0:512]
            pairs = [(E.wb[0][:, k * 1024 + m * 128: k * 1024 + m * 128 + 128],
                      E.xn[k][:, sl]) for k in range(NCH)]
            mm(P, ps_ap, pairs, True, True, reads=["wb0"] + xnk,
               writes=[ps_key], tag=f"l2q{m}_{tt}")
            dve(P, lambda v, p=ps_ap: v.tensor_scalar_mul(
                E.wk[11][:, 0:512], p, 0.03125),
                reads=[ps_key], writes=["wk11"], tag=f"l2qe{m}_{tt}")
            P.dma("qs", dram["qspill"][m * 128:(m + 1) * 128, sl],
                  E.wk[11][:, 0:512], reads=["wk11"], writes=["qspill"],
                  tag=f"l2qs{m}_{tt}")
    for wname, wkey, spname in (("l2_wk_t", "kspill", "kspill"),
                                ("l2_wv_t", "vspill", "vspill")):
        P.dma("qg", E.wb[0][:], ins_aps[wname], writes=["wb0"],
              tag=f"l2w{wname}")
        for kb in range(8):
            stage = E.wkbig[:, 5 * T:7 * T]
            for ch in range(NCH):
                rank = (kb * 512) // 2048
                lcol = (kb * 512) % 2048
                P.dma("qs", stage[:, ch * 512:(ch + 1) * 512],
                      dram["cc2_out"][rank * 1024 + ch * 128:
                                      rank * 1024 + (ch + 1) * 128,
                                      lcol:lcol + 512],
                      reads=["cc2_out"], writes=["wk5", "wk6"],
                      tag=f"l2s{wname}{kb}_{ch}")
            for m in range(8):
                ps_ap, ps_key = pA(E, m % 4)
                ps_ap = ps_ap[:, 0:512]
                pairs = [(E.wb[0][:, k * 1024 + m * 128: k * 1024 + m * 128 + 128],
                          stage[:, k * 512:(k + 1) * 512]) for k in range(NCH)]
                mm(P, ps_ap, pairs, True, True, reads=["wb0", "wk5"],
                   writes=[ps_key], tag=f"l2kv{wname}{kb}_{m}")
                dve(P, lambda v, p=ps_ap: v.tensor_copy(E.wk[11][:, 0:512], p),
                    reads=[ps_key], writes=["wk11"], tag=f"l2ke{wname}{kb}_{m}")
                P.dma("qs", dram[spname][m * 128:(m + 1) * 128,
                                         kb * 512:(kb + 1) * 512],
                      E.wk[11][:, 0:512], reads=["wk11"], writes=[spname],
                      tag=f"l2kp{wname}{kb}_{m}")
    for p in range(8):
        kv01 = E.wkbig[:, 0:2 * T]
        kv23 = E.wkbig[:, 2 * T:4 * T]
        P.dma("qs", kv01, dram["kspill"][p * 128:(p + 1) * 128, :],
              reads=["kspill"], writes=["wk0", "wk1"], tag=f"l2lk{p}")
        P.dma("qs", kv23, dram["vspill"][p * 128:(p + 1) * 128, :],
              reads=["vspill"], writes=["wk2", "wk3"], tag=f"l2lv{p}")
        P.dma("qs", E.wk[4], dram["qspill"][p * 128:(p + 1) * 128, :],
              reads=["qspill"], writes=["wk4"], tag=f"l2lq{p}")
        psT = E.psA[:, 0:512].bitcast(BF16)
        for hh in range(2):
            base = hh * 64
            dstw = E.wk[7 + hh]
            for kc in range(32):
                tr_in = kv23[base:base + 64, kc * 128:(kc + 1) * 128]
                transpose(P, psT[:, (kc % 8) * 64:(kc % 8) * 64 + 64], tr_in,
                          E.idn_bf[base:base + 64, base:base + 64],
                          reads=["wk2", "wk3", "idn_bf"], writes=["psA0"],
                          tag=f"l2vt{p}_{hh}_{kc}")
                if kc % 8 == 7:
                    g0 = kc - 7
                    dve(P, lambda v, d=dstw, g0=g0: v.tensor_copy(
                        d[:, g0 * 64:(g0 + 8) * 64], psT[:, 0:512]),
                        reads=["psA0"], writes=[f"wk{7 + hh}"],
                        tag=f"l2ve{p}_{hh}_{kc}")
        if _DEBUG and p == 0:
            tap(P, E, "dbg_vtm", E.wk[7][:, 0:2048], ["wk7"], (128, 2048),
                cast=True)
        for hh in range(2):
            base = hh * 64
            h2 = p * 2 + hh
            for qt in range(16):
                qsl = slice(qt * 128, (qt + 1) * 128)
                for kb in range(8):
                    ps_ap, ps_key = (pA(E, kb) if kb < 4 else pB(E, kb - 4))
                    ps_ap = ps_ap[:, 0:512]
                    mm(P, ps_ap, [(E.wk[4][base:base + 64, qsl],
                                   kv01[base:base + 64,
                                        kb * 512:(kb + 1) * 512])],
                       True, True, reads=["wk4", "wk0", "wk1"],
                       writes=[ps_key], tag=f"l2S{p}_{hh}_{qt}_{kb}")
                bt = E.wkbig[:, 9 * T:11 * T]
                for a in range(8):
                    Cc = 4095 - 256 * h2 - (qt * 8) - a
                    src = bass.AP(ins["relbf"], 16 * Cc, [[4096, 16], [1, 4096]])
                    P.unit("qa", lambda g, d=bt[a * 16:(a + 1) * 16, :], s=src:
                           g.dma_start(out=d, in_=s),
                           reads=["relbf"], writes=["wk9", "wk10"], inc=16,
                           tag=f"l2bt{p}_{hh}_{qt}_{a}")
                dve(P, lambda v: v.tensor_reduce(E.sf[1][:, 0:1], E.psA[:],
                                                 AX.X, ALU.max),
                    reads=["psA0", "psA1", "psA2", "psA3"], writes=["sf1"],
                    tag=f"l2mA{p}_{hh}_{qt}")
                dve(P, lambda v: v.tensor_reduce(E.sf[1][:, 1:2], E.psB[:],
                                                 AX.X, ALU.max),
                    reads=["psB0", "psB1", "psB2", "psB3"], writes=["sf1"],
                    tag=f"l2mB{p}_{hh}_{qt}")
                dve(P, lambda v: v.tensor_max(E.sf[1][:, 2:3], E.sf[1][:, 0:1],
                                              E.sf[1][:, 1:2]),
                    reads=["sf1"], writes=["sf1"], tag=f"l2mM{p}_{hh}_{qt}")
                dve(P, lambda v: v.tensor_scalar_mul(E.sf[1][:, 3:4],
                                                     E.sf[1][:, 2:3], -1.0),
                    reads=["sf1"], writes=["sf1"], tag=f"l2mN{p}_{hh}_{qt}")
                pw = E.wkbig[:, 5 * T:7 * T]
                act(P, pw[:, 0:2048], E.psA[:], AF.Exp, bias=E.sf[1][:, 3:4],
                    accum_out=E.sf[1][:, 4:5],
                    reads=["psA0", "psA1", "psA2", "psA3", "sf1"],
                    writes=["wk5", "wk6", "sf1"], tag=f"l2eA{p}_{hh}_{qt}")
                act(P, pw[:, 2048:4096], E.psB[:], AF.Exp, bias=E.sf[1][:, 3:4],
                    accum_out=E.sf[1][:, 5:6],
                    reads=["psB0", "psB1", "psB2", "psB3", "sf1"],
                    writes=["wk5", "wk6", "sf1"], tag=f"l2eB{p}_{hh}_{qt}")
                dve(P, lambda v: v.tensor_add(E.sf[1][:, 6:7], E.sf[1][:, 4:5],
                                              E.sf[1][:, 5:6]),
                    reads=["sf1"], writes=["sf1"], tag=f"l2sm{p}_{hh}_{qt}")
                dve(P, lambda v: v.reciprocal(E.sf[1][:, 7:8], E.sf[1][:, 6:7]),
                    reads=["sf1"], writes=["sf1"], tag=f"l2rc{p}_{hh}_{qt}")
                for hf in range(2):
                    dve(P, lambda v, hf=hf: v.tensor_single_scalar(
                        pw[:, hf * 2048:(hf + 1) * 2048],
                        pw[:, hf * 2048:(hf + 1) * 2048],
                        E.sf[1][:, 7:8], ALU.mult),
                        reads=["wk5", "wk6", "sf1"], writes=["wk5", "wk6"],
                        tag=f"l2pn{p}_{hh}_{qt}_{hf}")
                    if not _SKIP_BIAS:
                        dve(P, lambda v, hf=hf: v.tensor_add(
                            pw[:, hf * 2048:(hf + 1) * 2048],
                            pw[:, hf * 2048:(hf + 1) * 2048],
                            bt[:, hf * 2048:(hf + 1) * 2048]),
                            reads=["wk5", "wk6", "wk9", "wk10"],
                            writes=["wk5", "wk6"], tag=f"l2pb{p}_{hh}_{qt}_{hf}")
                if _DEBUG and p == 0 and hh == 0 and qt == 0:
                    tap(P, E, "dbg_pw", pw[:, 0:4096], ["wk5", "wk6"],
                        (128, 4096), cast=True)
                psT2 = E.psA[:, 0:512].bitcast(BF16)
                o_ps = E.psA[64 * 0:64, 512:640] if False else None
                for kc in range(32):
                    transpose(P, psT2[:, (kc % 8) * 128:(kc % 8) * 128 + 128],
                              pw[:, kc * 128:(kc + 1) * 128], E.idn_bf[:],
                              reads=["wk5", "wk6", "idn_bf"], writes=["psA0"],
                              tag=f"l2pt{p}_{hh}_{qt}_{kc}")
                    if kc % 8 == 7:
                        g0 = kc - 7
                        dve(P, lambda v, g0=g0: v.tensor_copy(
                            E.wk[11][:, (g0 % 16) * 128:(g0 % 16) * 128 + 1024],
                            psT2[:, 0:1024]),
                            reads=["psA0"], writes=["wk11"],
                            tag=f"l2pe{p}_{hh}_{qt}_{kc}")
                        if _DEBUG and p == 0 and hh == 0 and qt == 0 and kc == 7:
                            tap(P, E, "dbg_pt", E.wk[11][:, 0:1024],
                                ["wk11"], (128, 1024), cast=True)
                        for kc2 in range(g0, g0 + 8):
                            vtm = E.wk[7 + hh][:, kc2 * 64:(kc2 + 1) * 64]
                            pt = E.wk[11][:, (kc2 % 16) * 128:
                                          (kc2 % 16) * 128 + 128]
                            mm(P, E.psB[base:base + 64, 0:128],
                               [(vtm, pt)], start=(kc2 == 0),
                               stop=(kc2 == 31),
                               reads=[f"wk{7 + hh}", "wk11"], writes=["psB0"],
                               tag=f"l2pv{p}_{hh}_{qt}_{kc2}",
                               accumulate=(kc2 > 0), tp=(0, base))
                dve(P, lambda v, base=base: v.tensor_copy(
                    E.sf[0][base:base + 64, 0:128],
                    E.psB[base:base + 64, 0:128]),
                    reads=["psB0"], writes=["sf0"], tag=f"l2oe{p}_{hh}_{qt}")
                if _DEBUG and p == 0 and hh == 0 and qt == 0:
                    tap(P, E, "dbg_o00", E.sf[0][0:64, 0:128], ["sf0"],
                        (64, 128))
                P.dma("qs", dram["sspill"][p * 128 + base:p * 128 + base + 64,
                                           qsl],
                      E.sf[0][base:base + 64, 0:128], reads=["sf0"],
                      writes=["sspill"], tag=f"l2os{p}_{hh}_{qt}")


def emit_l2_gatemask(P, E, ins, ins_aps, dram):
    P.dma("qs", E.wrf[:], ins_aps["l2_wr_f"], writes=["wrf"], tag="l2gwf")
    P.dma("qs", row(E, 0)[:, 0:8], ins_aps["l2_br_row"], reads=[], writes=[R_ROWS],
          tag="l2gbr")
    mm(P, E.psB[0:8, 0:1], [(E.wrf[:, k * 8:(k + 1) * 8], E.ones_f[:, 0:1])
                            for k in range(NCH)], True, True,
       reads=["wrf", "ones_f"], writes=["psB0"], tag="l2gws")
    dve(P, lambda v: v.tensor_copy(E.sf[1][0:8, 0:1], E.psB[0:8, 0:1]),
        reads=["psB0"], writes=["sf1"], tag="l2gwe")
    transpose(P, E.psB[0:1, 0:8], E.sf[1][0:8, 0:1], E.idn_f[0:8, 0:8],
              reads=["sf1", "idn_f"], writes=["psB0"], tag="l2gwt")
    dve(P, lambda v: v.tensor_copy(row(E, 0)[:, 8:16], E.psB[0:1, 0:8]),
        reads=["psB0"], writes=[R_ROWS], tag="l2gwr")
    mm(P, E.psB[:, 512:520], [(E.onesr_f[0:1, :], row(E, 0)[:, 8:16])],
       True, True, reads=[R_ROWS, "onesr_f"], writes=["psB1"], tag="l2gwb")
    mm(P, E.psB[:, 520:528], [(E.onesr_f[0:1, :], row(E, 0)[:, 0:8])],
       True, True, reads=[R_ROWS, "onesr_f"], writes=["psB1"], tag="l2gbb")
    for tt in range(4):
        sl = slice(tt * 512, (tt + 1) * 512)
        a_ap, a_key = pA(E, tt)
        for ch in range(NCH):
            P.dma("qs", E.sf[0][:, 0:512],
                  dram["xio"][ch * 128:(ch + 1) * 128, sl],
                  reads=[dram["xio_key"]], writes=["sf0"], tag=f"l2gx{tt}_{ch}")
            mm(P, a_ap[0:8, :], [(E.wrf[:, ch * 8:(ch + 1) * 8],
                                  E.sf[0][:, 0:512])],
               start=(ch == 0), stop=(ch == NCH - 1),
               reads=["wrf", "sf0"], writes=[a_key], tag=f"l2gm{tt}_{ch}",
               accumulate=(ch > 0))
    dve(P, lambda v: v.tensor_copy(E.sf[0][0:8, :], E.psA[0:8, :]),
        reads=["psA0", "psA1", "psA2", "psA3"], writes=["sf0"], tag="l2gle")
    for ti in range(16):
        csl = slice(ti * 128, (ti + 1) * 128)
        transpose(P, E.psB[0:128, 8:9], row(E, 1)[:, csl], E.idn_f[0:1, 0:1],
                  reads=[R_ROWS, "idn_f"], writes=["psB0"], tag=f"l2ga{ti}")
        dve(P, lambda v: v.tensor_copy(E.sf[1][:, 8:9], E.psB[0:128, 8:9]),
            reads=["psB0"], writes=["sf1"], tag=f"l2gae{ti}")
        transpose(P, E.psB[0:128, 8:9], row(E, 2)[:, csl], E.idn_f[0:1, 0:1],
                  reads=[R_ROWS, "idn_f"], writes=["psB0"], tag=f"l2gb{ti}")
        dve(P, lambda v: v.tensor_copy(E.sf[1][:, 9:10], E.psB[0:128, 8:9]),
            reads=["psB0"], writes=["sf1"], tag=f"l2gbe{ti}")
        transpose(P, E.psB[0:128, 0:8], E.sf[0][0:8, csl], E.idn_f[0:8, 0:8],
                  reads=["sf0", "idn_f"], writes=["psB0"], tag=f"l2gt{ti}")
        lt = E.sf[1][:, 16:24]
        dve(P, lambda v, a=lt: v.tensor_copy(a, E.psB[0:128, 0:8]),
            reads=["psB0"], writes=["sf1"], tag=f"l2gte{ti}")
        l1 = E.sf[1][:, 24:32]
        dve(P, lambda v, a=l1, b=lt: v.tensor_single_scalar(
            a, b, E.sf[1][:, 8:9], ALU.mult),
            reads=["sf1"], writes=["sf1"], tag=f"l2g1{ti}")
        l2_ = E.sf[1][:, 32:40]
        dve(P, lambda v, a=l2_: v.tensor_single_scalar(
            a, E.psB[:, 512:520], E.sf[1][:, 9:10], ALU.mult),
            reads=["psB1", "sf1"], writes=["sf1"], tag=f"l2g2{ti}")
        dve(P, lambda v, a=l1, b=l2_: v.tensor_add(a, a, b),
            reads=["sf1"], writes=["sf1"], tag=f"l2g3{ti}")
        dve(P, lambda v, a=l1: v.tensor_add(a, a, E.psB[:, 520:528]),
            reads=["psB1", "sf1"], writes=["sf1"], tag=f"l2g4{ti}")
        m1 = E.sf[1][:, 10:11]
        dve(P, lambda v, a=m1, b=l1: v.tensor_reduce(a, b, AX.X, ALU.max),
            reads=["sf1"], writes=["sf1"], tag=f"l2g5{ti}")
        dve(P, lambda v, a=m1: v.tensor_scalar_mul(a, a, -1.0),
            reads=["sf1"], writes=["sf1"], tag=f"l2g6{ti}")
        ee = E.sf[1][:, 40:48]
        sm = E.sf[1][:, 11:12]
        act(P, ee, l1, AF.Exp, bias=m1, accum_out=sm,
            reads=["sf1"], writes=["sf1"], tag=f"l2g7{ti}")
        dve(P, lambda v, a=sm: v.reciprocal(a, a),
            reads=["sf1"], writes=["sf1"], tag=f"l2g8{ti}")
        dve(P, lambda v, a=ee, b=sm: v.tensor_single_scalar(a, a, b, ALU.mult),
            reads=["sf1"], writes=["sf1"], tag=f"l2g9{ti}")
        msk = E.sf[1][:, 48:56]
        dve(P, lambda v, a=msk, b=ee: v.tensor_single_scalar(
            a, b, 0.5, ALU.is_gt),
            reads=["sf1"], writes=["sf1"], tag=f"l2gA{ti}")
        transpose(P, E.psB[0:8, 0:128], msk, E.idn_f[:],
                  reads=["sf1", "idn_f"], writes=["psB0"], tag=f"l2gB{ti}")
        dve(P, lambda v, csl=csl: v.tensor_copy(E.sf[0][32:40, csl],
                                                E.psB[0:8, 0:128]),
            reads=["psB0"], writes=["sf0"], tag=f"l2gC{ti}")
    P.dma("qs", dram["maskspill"], E.sf[0][32:40, :], reads=["sf0"],
          writes=["maskspill"], tag="l2gS")


def emit_l2_ffn_residual(P, E, ins, ins_aps, dram):
    xnk = [f"xn{c}" for c in range(NCH)]
    P.dma("qs", E.sf[0][0:8, :], dram["maskspill"], reads=["maskspill"],
          writes=["sf0"], tag="l2ml")
    if _DEBUG:
        tap(P, E, "dbg_mask", E.sf[0][0:8, :], ["sf0"], (8, T))
    P.dma("qs", E.b2p[:], ins_aps["l2_bep"], writes=["b2p"], tag="l2be")
    TQ = 256
    for tq in range(T // TQ):
        tsl = slice(tq * TQ, (tq + 1) * TQ)
        for e in range(8):
            P.dma("qg", E.wb[0][:], bass.AP(
                ins["l2_we_t"], e * 8192, [[8 * 8192, 128], [1, 8192]]),
                writes=["wb0"], tag=f"l2we{tq}_{e}")
            P.dma("qs", row(E, 1)[:, 0:TQ], E.sf[0][e:e + 1, tsl],
                  reads=["sf0"], writes=[R_ROWS], tag=f"l2mr{tq}_{e}")
            mm(P, E.psB[:, 512:768], [(E.onesr_f[0:1, :], row(E, 1)[:, 0:TQ])],
               True, True, reads=[R_ROWS, "onesr_f"], writes=["psB1"],
               tag=f"l2mb{tq}_{e}")
            xm = E.wkbig[:, 8 * T:9 * T]
            for ch in range(NCH):
                dve(P, lambda v, ch=ch, tsl=tsl: v.tensor_mul(
                    xm[:, ch * TQ:(ch + 1) * TQ], E.xn[ch][:, tsl],
                    E.psB[:, 512:768]),
                    reads=[f"xn{ch}", "psB1"], writes=["wk8"],
                    tag=f"l2xm{tq}_{e}_{ch}")
            if _DEBUG and tq == 0 and e == 0:
                tap(P, E, "dbg_xm", xm[:, 0:2048], ["wk8"], (128, 2048),
                    cast=True)
            for m in range(8):
                ps_y = E.psA[:, (m // 2) * 512 + (m % 2) * TQ:
                             (m // 2) * 512 + (m % 2) * TQ + TQ]
                ps_yk = f"psA{m // 2}"
                pairs = [(E.wb[0][:, k * 1024 + m * 128: k * 1024 + m * 128 + 128],
                          xm[:, k * TQ:(k + 1) * TQ]) for k in range(NCH)]
                mm(P, ps_y, pairs, start=(e == 0 and m % 2 == 0), stop=False,
                   reads=["wb0", "wk8"], writes=[ps_yk], tag=f"l2y{tq}_{e}_{m}",
                   accumulate=not (e == 0 and m % 2 == 0))
        for m in range(8):
            ps_y = E.psA[:, (m // 2) * 512 + (m % 2) * TQ:
                         (m // 2) * 512 + (m % 2) * TQ + TQ]
            ps_yk = f"psA{m // 2}"
            mm(P, ps_y, [(E.b2p[:, m * 128:(m + 1) * 128], E.sf[0][0:8, tsl])],
               False, True, reads=["b2p", "sf0"], writes=[ps_yk],
               tag=f"l2bb{tq}_{m}", accumulate=True)
            gsl = E.sf[1][:, 0:TQ]
            act(P, gsl, ps_y, AF.Silu, reads=[ps_yk], writes=["sf1"],
                tag=f"l2gg{tq}_{m}")
            if _DEBUG and tq == 0 and m == 0:
                tap(P, E, "dbg_g00", gsl, ["sf1"], (128, 256))
            ssl = E.sf[1][:, 256:256 + TQ]
            P.dma("qs", ssl, dram["sspill"][m * 128:(m + 1) * 128, tsl],
                  reads=["sspill"], writes=["sf1"], tag=f"l2sl{tq}_{m}")
            xsl = E.sf[1][:, 512:512 + TQ]
            P.dma("qs", xsl, dram["xio"][m * 128:(m + 1) * 128, tsl],
                  reads=[dram["xio_key"]], writes=["sf1"], tag=f"l2xl{tq}_{m}")
            dve(P, lambda v, g=gsl, s_=ssl: v.tensor_mul(g, s_, g),
                reads=["sf1"], writes=["sf1"], tag=f"l2sg{tq}_{m}")
            dve(P, lambda v, g=gsl, x_=xsl: v.tensor_add(x_, x_, g),
                reads=["sf1"], writes=["sf1"], tag=f"l2xa{tq}_{m}")
            P.dma("qs", dram["xio2"][m * 128:(m + 1) * 128, tsl], xsl,
                  reads=["sf1"], writes=[dram["xio2_key"]], tag=f"l2xs{tq}_{m}")


def emit_layer2(P, E, ins, ins_aps, dram):
    emit_ln(P, E, dram["xio"], dram["xio_key"], E.xn,
            [f"xn{c}" for c in range(NCH)], rowbase=0, tag="l2ln")
    emit_l2_gatemask(P, E, ins, ins_aps, dram)
    emit_l2_attn(P, E, ins, ins_aps, dram)
    if _DEBUG:
        t = P.nc.dram_tensor("dbg_preln", [D, T], F32, kind="ExternalOutput")
        P.dma("qs", t.ap(), dram["sspill"], reads=["sspill"], tag="dbg_preln")
        t2 = P.nc.dram_tensor("dbg_q", [D, T], F32, kind="ExternalOutput")
        P.dma("qg", t2.ap(), dram["qspill"], reads=["qspill"], tag="dbg_q")
        t3 = P.nc.dram_tensor("dbg_k", [D, 2 * T], F32, kind="ExternalOutput")
        P.dma("qg", t3.ap(), dram["kspill"], reads=["kspill"], tag="dbg_k")
    g_aps = [E.scal[:, 16 + c:17 + c] for c in range(NCH)]
    b_aps = [E.scal[:, 24 + c:25 + c] for c in range(NCH)]
    P.dma("qs", E.scal[:, 16:24], ins_aps["l2_gamma"], writes=["scal_gb"],
          tag="l2gma")
    P.dma("qs", E.scal[:, 24:32], ins_aps["l2_beta"], writes=["scal_gb"],
          tag="l2bta")
    emit_ln(P, E, dram["sspill"], "sspill", None, None, rowbase=0,
            gb=(g_aps, b_aps, ["scal_gb"]),
            out_dram=(dram["sspill"], "sspill"), tag="l2pln")
    emit_l2_ffn_residual(P, E, ins, ins_aps, dram)


def host_l2_weights(I, parity):
    w = {}
    w["l2_wq_t"] = pack_lhsT(I["L2_attn_Wq"])
    w["l2_wk_t"] = pack_lhsT(I["L2_attn_Wk"])
    w["l2_wv_t"] = pack_lhsT(I["L2_attn_Wv"])
    idx = np.clip(np.arange(8192) - 128 * parity, 0, 2046)
    w["l2_relpad"] = np.ascontiguousarray(I["L2_attn_rel_table"][idx])
    w["l2_gamma"] = pack_bias(I["L2_attn_gamma"])
    w["l2_beta"] = pack_bias(I["L2_attn_beta"])
    wr = pack_lhsT(np.ascontiguousarray(I["L2_ffn_Wr"].T))
    w["l2_wr_f"] = np.pad(wr, ((0, 0), (0, 64 - wr.shape[1])))
    w["l2_br_row"] = np.ascontiguousarray(I["L2_ffn_br"].reshape(1, 8))
    w["l2_we_t"] = np.concatenate(
        [pack_lhsT(np.ascontiguousarray(I["L2_ffn_We"][e].T)) for e in range(8)],
        axis=1)
    w["l2_bep"] = np.ascontiguousarray(I["L2_ffn_be"])
    return w



def emit_l1_attn(P, E, ins, ins_aps, dram):
    xnk = [f"xn{c}" for c in range(NCH)]
    for ch in range(NCH):
        P.dma("qg", dram["cc2_in"][ch * 128:(ch + 1) * 128, :], E.xn[ch][:],
              reads=[f"xn{ch}"], writes=["cc2_in"], tag=f"l1agw{ch}")
    P.unit("cc", lambda g: g.collective_compute(
        "AllGather", ALU.bypass, replica_groups=[[0, 1], [2, 3], [4, 5], [6, 7]],
        ins=[dram["cc2_in"].opt()], outs=[dram["cc2_out"].opt()]),
        reads=["cc2_in"], writes=["cc2_out"], inc=1, tag="l1ag")
    P.dma("qs", E.scal[:, 32:40], ins_aps["l1_scale_p"], writes=["scal_l1s"],
          tag="l1sc")
    act(P, E.scal[:, 40:48], E.scal[:, 32:40], AF.Exp,
        reads=["scal_l1s"], writes=["scal_l1s"], tag="l1sce")

    def l2norm_evac(ps_ap, ps_key, pair, dst_ap, dst_key, scale_col, tag):
        sq = E.wk[11][:, 1024:1536]
        act(P, sq, ps_ap, AF.Square, reads=[ps_key], writes=["wk11"],
            tag=tag + "sq")
        mm(P, E.psB[0:2, 512:1024], [(E.half_bf[:], sq)], True, True,
           reads=["wk11", "half_bf"], writes=["psB1"], tag=tag + "nm")
        nr = E.sf[1][0:2, 0:512]
        dve(P, lambda v: v.tensor_copy(nr, E.psB[0:2, 512:1024]),
            reads=["psB1"], writes=["sf1"], tag=tag + "ne")
        dve(P, lambda v: v.tensor_single_scalar(nr, nr, 1e-24, ALU.max),
            reads=["sf1"], writes=["sf1"], tag=tag + "nx")
        act(P, nr, nr, AF.Ln, reads=["sf1"], writes=["sf1"], tag=tag + "nl")
        act(P, nr, nr, AF.Exp, scale=E.misc[0:2, 1:2],
            reads=["sf1", "misc"], writes=["sf1"], tag=tag + "nE")
        if scale_col is not None:
            dve(P, lambda v: v.tensor_single_scalar(
                nr, nr, E.scal[0:2, 40 + pair:41 + pair], ALU.mult),
                reads=["sf1", "scal_l1s"], writes=["sf1"], tag=tag + "ns")
        mm(P, E.psB[:, 512:1024], [(E.halfsel_f[0:2, :], nr)], True, True,
           reads=["sf1", "halfsel_f"], writes=["psB1"], tag=tag + "nb")
        q_sb = E.wk[11][:, 0:512]
        dve(P, lambda v: v.tensor_copy(q_sb, ps_ap),
            reads=[ps_key], writes=["wk11"], tag=tag + "qe")
        dve(P, lambda v: v.tensor_mul(dst_ap, q_sb, E.psB[:, 512:1024]),
            reads=["wk11", "psB1"], writes=[dst_key], tag=tag + "qm")

    P.dma("qg", E.wb[0][:], ins_aps["l1_wq_t"], writes=["wb0"], tag="l1wq")
    for m in range(8):
        for tt in range(4):
            sl = slice(tt * 512, (tt + 1) * 512)
            ps_ap, ps_key = pA(E, (m * 4 + tt) % 4)
            ps_ap = ps_ap[:, 0:512]
            pairs = [(E.wb[0][:, k * 1024 + m * 128: k * 1024 + m * 128 + 128],
                      E.xn[k][:, sl]) for k in range(NCH)]
            mm(P, ps_ap, pairs, True, True, reads=["wb0"] + xnk,
               writes=[ps_key], tag=f"l1q{m}_{tt}")
            l2norm_evac(ps_ap, ps_key, m, E.wk[10][:, 0:512], "wk10", True,
                        f"l1qn{m}_{tt}")
            P.dma("qs", dram["qspill"][m * 128:(m + 1) * 128, sl],
                  E.wk[10][:, 0:512], reads=["wk10"], writes=["qspill"],
                  tag=f"l1qs{m}_{tt}")
    for wname, spname, norm in (("l1_wk_t", "kspill", True),
                                ("l1_wv_t", "vspill", False)):
        P.dma("qg", E.wb[0][:], ins_aps[wname], writes=["wb0"], tag=f"l1w{spname}")
        for kb in range(8):
            stage = E.wkbig[:, 5 * T:7 * T]
            for ch in range(NCH):
                rank = (kb * 512) // 2048
                lcol = (kb * 512) % 2048
                P.dma("qs", stage[:, ch * 512:(ch + 1) * 512],
                      dram["cc2_out"][rank * 1024 + ch * 128:
                                      rank * 1024 + (ch + 1) * 128,
                                      lcol:lcol + 512],
                      reads=["cc2_out"], writes=["wk5", "wk6"],
                      tag=f"l1s{spname}{kb}_{ch}")
            for m in range(8):
                ps_ap, ps_key = pA(E, m % 4)
                ps_ap = ps_ap[:, 0:512]
                pairs = [(E.wb[0][:, k * 1024 + m * 128: k * 1024 + m * 128 + 128],
                          stage[:, k * 512:(k + 1) * 512]) for k in range(NCH)]
                mm(P, ps_ap, pairs, True, True, reads=["wb0", "wk5", "wk6"],
                   writes=[ps_key], tag=f"l1kv{spname}{kb}_{m}")
                if norm:
                    l2norm_evac(ps_ap, ps_key, m, E.wk[10][:, 0:512], "wk10",
                                None, f"l1kn{spname}{kb}_{m}")
                else:
                    dve(P, lambda v, p=ps_ap: v.tensor_copy(
                        E.wk[10][:, 0:512], p),
                        reads=[ps_key], writes=["wk10"],
                        tag=f"l1ke{spname}{kb}_{m}")
                P.dma("qs", dram[spname][m * 128:(m + 1) * 128,
                                         kb * 512:(kb + 1) * 512],
                      E.wk[10][:, 0:512], reads=["wk10"], writes=[spname],
                      tag=f"l1kp{spname}{kb}_{m}")
    for p in range(8):
        kv01 = E.wkbig[:, 0:2 * T]
        kv23 = E.wkbig[:, 2 * T:4 * T]
        P.dma("qs", kv01, dram["kspill"][p * 128:(p + 1) * 128, :],
              reads=["kspill"], writes=["wk0", "wk1"], tag=f"l1lk{p}")
        P.dma("qs", kv23, dram["vspill"][p * 128:(p + 1) * 128, :],
              reads=["vspill"], writes=["wk2", "wk3"], tag=f"l1lv{p}")
        P.dma("qs", E.wk[4][:, :], dram["qspill"][p * 128:(p + 1) * 128, :],
              reads=["qspill"], writes=["wk4"], tag=f"l1lq{p}")
        psT = E.psA[:, 0:512].bitcast(BF16)
        for hh in range(2):
            base = hh * 64
            dstw = E.wk[7 + hh]
            for kc in range(32):
                tr_in = kv23[base:base + 64, kc * 128:(kc + 1) * 128]
                transpose(P, psT[:, (kc % 8) * 64:(kc % 8) * 64 + 64], tr_in,
                          E.idn_bf[base:base + 64, base:base + 64],
                          reads=["wk2", "wk3", "idn_bf"], writes=["psA0"],
                          tag=f"l1vt{p}_{hh}_{kc}")
                if kc % 8 == 7:
                    g0 = kc - 7
                    dve(P, lambda v, d=dstw, g0=g0: v.tensor_copy(
                        d[:, g0 * 64:(g0 + 8) * 64], psT[:, 0:512]),
                        reads=["psA0"], writes=[f"wk{7 + hh}"],
                        tag=f"l1ve{p}_{hh}_{kc}")
        for hh in range(2):
            base = hh * 64
            for qt in range(16):
                qsl = slice(qt * 128, (qt + 1) * 128)
                for kb in range(8):
                    ps_ap, ps_key = (pA(E, kb) if kb < 4 else pB(E, kb - 4))
                    ps_ap = ps_ap[:, 0:512]
                    mm(P, ps_ap, [(E.wk[4][base:base + 64, qsl],
                                   kv01[base:base + 64,
                                        kb * 512:(kb + 1) * 512])],
                       True, True, reads=["wk4", "wk0", "wk1"],
                       writes=[ps_key], tag=f"l1S{p}_{hh}_{qt}_{kb}")
                pw = E.wkbig[:, 5 * T:7 * T]
                dve(P, lambda v: v.tensor_reduce(E.sf[1][:, 0:1], E.psA[:],
                                                 AX.X, ALU.max),
                    reads=["psA0", "psA1", "psA2", "psA3"], writes=["sf1"],
                    tag=f"l1mA{p}_{hh}_{qt}")
                dve(P, lambda v: v.tensor_reduce(E.sf[1][:, 1:2], E.psB[:],
                                                 AX.X, ALU.max),
                    reads=["psB0", "psB1", "psB2", "psB3"], writes=["sf1"],
                    tag=f"l1mB{p}_{hh}_{qt}")
                dve(P, lambda v: v.tensor_max(E.sf[1][:, 2:3], E.sf[1][:, 0:1],
                                              E.sf[1][:, 1:2]),
                    reads=["sf1"], writes=["sf1"], tag=f"l1mM{p}_{hh}_{qt}")
                dve(P, lambda v: v.tensor_scalar_mul(E.sf[1][:, 3:4],
                                                     E.sf[1][:, 2:3], -1.0),
                    reads=["sf1"], writes=["sf1"], tag=f"l1mN{p}_{hh}_{qt}")
                act(P, pw[:, 0:2048], E.psA[:], AF.Exp, bias=E.sf[1][:, 3:4],
                    accum_out=E.sf[1][:, 4:5],
                    reads=["psA0", "psA1", "psA2", "psA3", "sf1"],
                    writes=["wk5", "wk6", "sf1"], tag=f"l1eA{p}_{hh}_{qt}")
                act(P, pw[:, 2048:4096], E.psB[:], AF.Exp, bias=E.sf[1][:, 3:4],
                    accum_out=E.sf[1][:, 5:6],
                    reads=["psB0", "psB1", "psB2", "psB3", "sf1"],
                    writes=["wk5", "wk6", "sf1"], tag=f"l1eB{p}_{hh}_{qt}")
                dve(P, lambda v: v.tensor_add(E.sf[1][:, 6:7], E.sf[1][:, 4:5],
                                              E.sf[1][:, 5:6]),
                    reads=["sf1"], writes=["sf1"], tag=f"l1sm{p}_{hh}_{qt}")
                dve(P, lambda v: v.reciprocal(E.sf[1][:, 7:8], E.sf[1][:, 6:7]),
                    reads=["sf1"], writes=["sf1"], tag=f"l1rc{p}_{hh}_{qt}")
                for hf in range(2):
                    dve(P, lambda v, hf=hf: v.tensor_single_scalar(
                        pw[:, hf * 2048:(hf + 1) * 2048],
                        pw[:, hf * 2048:(hf + 1) * 2048],
                        E.sf[1][:, 7:8], ALU.mult),
                        reads=["wk5", "wk6", "sf1"], writes=["wk5", "wk6"],
                        tag=f"l1pn{p}_{hh}_{qt}_{hf}")
                psT2 = E.psA[:, 0:512].bitcast(BF16)
                for kc in range(32):
                    transpose(P, psT2[:, (kc % 8) * 128:(kc % 8) * 128 + 128],
                              pw[:, kc * 128:(kc + 1) * 128], E.idn_bf[:],
                              reads=["wk5", "wk6", "idn_bf"], writes=["psA0"],
                              tag=f"l1pt{p}_{hh}_{qt}_{kc}")
                    if kc % 8 == 7:
                        g0 = kc - 7
                        dve(P, lambda v, g0=g0: v.tensor_copy(
                            E.wk[11][:, (g0 % 16) * 128:(g0 % 16) * 128 + 1024],
                            psT2[:, 0:1024]),
                            reads=["psA0"], writes=["wk11"],
                            tag=f"l1pe{p}_{hh}_{qt}_{kc}")
                        for kc2 in range(g0, g0 + 8):
                            vtm = E.wk[7 + hh][:, kc2 * 64:(kc2 + 1) * 64]
                            pt = E.wk[11][:, (kc2 % 16) * 128:
                                          (kc2 % 16) * 128 + 128]
                            mm(P, E.psB[base:base + 64, 0:128],
                               [(vtm, pt)], start=(kc2 == 0),
                               stop=(kc2 == 31),
                               reads=[f"wk{7 + hh}", "wk11"], writes=["psB0"],
                               tag=f"l1pv{p}_{hh}_{qt}_{kc2}",
                               accumulate=(kc2 > 0), tp=(0, base))
                dve(P, lambda v, base=base: v.tensor_copy(
                    E.sf[0][base:base + 64, 0:128],
                    E.psB[base:base + 64, 0:128]),
                    reads=["psB0"], writes=["sf0"], tag=f"l1oe{p}_{hh}_{qt}")
                def st_o(eng, p=p, base=base, qsl=qsl):
                    return eng.dma_start(
                        out=dram["osp"][p * 128 + base:p * 128 + base + 64, qsl],
                        in_=E.sf[0][base:base + 64, 0:128])
                P.unit("qg", st_o, reads=["sf0"], writes=["osp"], inc=16,
                       tag=f"l1os{p}_{hh}_{qt}")
    P.dma("qg", E.wb[0][:], ins_aps["l1_wo_t"], writes=["wb0"], tag="l1wo")
    P.dma("qs", E.scal[:, 0:8], ins_aps["l1_bo"], writes=["scal_lo"], tag="l1bo")
    for tt in range(4):
        sl = slice(tt * 512, (tt + 1) * 512)
        for ch in range(NCH):
            half = E.wk[8] if ch < 4 else E.wk[9]
            dst = half[:, (ch % 4) * 512:(ch % 4 + 1) * 512]
            P.dma("qs", dst, dram["osp"][ch * 128:(ch + 1) * 128, sl],
                  reads=["osp"], writes=[f"wk{8 + ch // 4}"],
                  tag=f"l1ol{tt}_{ch}")
        for m in range(8):
            ps_ap, ps_key = (pA(E, m % 4) if m < 4 else pB(E, m % 4))
            ps_ap = ps_ap[:, 0:512]
            pairs = []
            for k in range(NCH):
                half = E.wk[8] if k < 4 else E.wk[9]
                pairs.append((E.wb[0][:, k * 1024 + m * 128: k * 1024 + m * 128 + 128],
                              half[:, (k % 4) * 512:(k % 4 + 1) * 512]))
            mm(P, ps_ap, pairs, True, True,
               reads=["wb0", "wk8", "wk9"], writes=[ps_key], tag=f"l1o{tt}_{m}")
            dve(P, lambda v, m=m, p=ps_ap: v.tensor_scalar_add(
                E.sf[0][:, 0:512], p, E.scal[:, m:m + 1]),
                reads=[ps_key, "scal_lo"], writes=["sf0"], tag=f"l1oe2{tt}_{m}")
            P.dma("qs", dram["sspill"][m * 128:(m + 1) * 128, sl],
                  E.sf[0][:, 0:512], reads=["sf0"], writes=["sspill"],
                  tag=f"l1ss{tt}_{m}")


def host_l1_weights(I):
    w = {}
    w["l1_wq_t"] = pack_lhsT(I["L1_attn_w_q"])
    wkv = I["L1_attn_w_kv"]
    w["l1_wk_t"] = pack_lhsT(np.ascontiguousarray(wkv[:, 0:1024]))
    w["l1_wv_t"] = pack_lhsT(np.ascontiguousarray(wkv[:, 1024:2048]))
    w["l1_wo_t"] = pack_lhsT(I["L1_attn_w_out"])
    w["l1_bo"] = pack_bias(I["L1_attn_b_out"])
    sc = np.zeros((128, 8), np.float32)
    s16 = np.asarray(I["L1_attn_scale"]).reshape(16)
    for h in range(16):
        sc[h % 2, h // 2] = s16[h]
    w["l1_scale_p"] = sc
    return w



HP = 2816
NH1 = 44


def emit_l1_ffn_residual(P, E, ins, ins_aps, dram):
    xnk = [f"xn{c}" for c in range(NCH)]
    for ch in range(NCH):
        P.dma("qg", dram["ix1"][ch * 128:(ch + 1) * 128, :], E.xn[ch][:],
              reads=[f"xn{ch}"], writes=["ix1"], tag=f"l1f_i{ch}")
    ix, ix2 = dram["ix1"], dram["ix2"]
    ixk, ix2k = "ix1", "ix2"
    for li in range(6):
        P.dma("qs", E.scal[:, 0:1],
              bass.AP(ins["l1f_qs"], li * 128, [[1, 128], [1, 1]]),
              writes=["scal_lo"], tag=f"f{li}qs")
        P.dma("qs", E.scal[:, 1:2],
              bass.AP(ins["l1f_ks"], li * 128, [[1, 128], [1, 1]]),
              writes=["scal_hi"], tag=f"f{li}ks")
        for ch in range(NCH):
            P.dma("qs", dram["cc3_in"][ch * 128:(ch + 1) * 128, :],
                  ix[ch * 128:(ch + 1) * 128, 1791:2048],
                  reads=[ixk], writes=["cc3_in"], tag=f"f{li}ta{ch}")
        P.unit("cc", lambda g: g.collective_compute(
            "AllGather", ALU.bypass,
            replica_groups=[[0, 1], [2, 3], [4, 5], [6, 7]],
            ins=[dram["cc3_in"].opt()], outs=[dram["cc3_out"].opt()]),
            reads=["cc3_in"], writes=["cc3_out"], inc=1, tag=f"f{li}ag")
        spx = E.wk[5]
        for ch in range(NCH):
            lo = 1 if ch < 4 else 0
            P.dma("qs", E.sf[0][:, 0:256],
                  dram["cc3_out"][ch * 128:(ch + 1) * 128, lo:lo + 256],
                  reads=["cc3_out"], writes=["sf0"], tag=f"f{li}p0{ch}")
            P.dma("qs", E.sf[0][:, 256:512],
                  dram["cc3_out"][1024 + ch * 128:1024 + (ch + 1) * 128,
                                  lo:lo + 256],
                  reads=["cc3_out"], writes=["sf0"], tag=f"f{li}p1{ch}")
            dve(P, lambda v: v.tensor_single_scalar(
                E.sf[0][:, 0:256], E.sf[0][:, 0:256], E.selp[:, 0:1], ALU.mult),
                reads=["sf0", "selp"], writes=["sf0"], tag=f"f{li}ps{ch}")
            dve(P, lambda v, ch=ch: v.scalar_tensor_tensor(
                spx[:, ch * 256:(ch + 1) * 256], E.sf[0][:, 256:512],
                E.selp[:, 1:2], E.sf[0][:, 0:256], ALU.mult, ALU.add),
                reads=["sf0", "selp"], writes=["wk5"], tag=f"f{li}pc{ch}")
        for ch in range(NCH):
            if ch < 4:
                P.dma("qg", E.xn[ch][:], ix[ch * 128:(ch + 1) * 128, :],
                      reads=[ixk], writes=[f"xn{ch}"], tag=f"f{li}sx{ch}")
            else:
                P.dma("qg", E.xn[ch][:, 1:2048],
                      ix[ch * 128:(ch + 1) * 128, 0:2047],
                      reads=[ixk], writes=[f"xn{ch}"], tag=f"f{li}sx{ch}")
                P.dma("qs", E.sf[0][:, 0:1],
                      dram["cc3_out"][ch * 128:(ch + 1) * 128, 256:257],
                      reads=["cc3_out"], writes=["sf0"], tag=f"f{li}c0{ch}",
                      slow=True)
                P.dma("qs", E.sf[0][:, 1:2],
                      dram["cc3_out"][1024 + ch * 128:1024 + (ch + 1) * 128,
                                      256:257],
                      reads=["cc3_out"], writes=["sf0"], tag=f"f{li}c1{ch}",
                      slow=True)
                dve(P, lambda v: v.tensor_single_scalar(
                    E.sf[0][:, 0:1], E.sf[0][:, 0:1], E.selp[:, 0:1], ALU.mult),
                    reads=["sf0", "selp"], writes=["sf0"], tag=f"f{li}cs{ch}")
                dve(P, lambda v, ch=ch: v.scalar_tensor_tensor(
                    E.xn[ch][:, 0:1], E.sf[0][:, 1:2], E.selp[:, 1:2],
                    E.sf[0][:, 0:1], ALU.mult, ALU.add),
                    reads=["sf0", "selp"], writes=[f"xn{ch}"], tag=f"f{li}cc{ch}")
        def qkv_block(src_chs, src_keys, csl0, blkw, rot_in, rot_key, dst,
                      dst_key, kind, tagb):
            wbuf, wkey2 = {"q": (E.wb[0], "wb0"), "k": (E.wb[1], "wb1"),
                           "v": (E.wbq, "wbq")}[kind]
            for m in range(8):
                ps_ap, ps_key = pA(E, m % 2)
                ps_ap = ps_ap[:, 0:blkw]
                pairs = [(wbuf[:, k * 1024 + m * 128: k * 1024 + m * 128 + 128],
                          src_chs[k]) for k in range(NCH)]
                mm(P, ps_ap, pairs, True, True,
                   reads=[wkey2] + src_keys, writes=[ps_key],
                   tag=f"{tagb}m{m}")
                dpair = dst[:, m * blkw:(m + 1) * blkw]
                if kind == "v":
                    dve(P, lambda v, d=dpair, p=ps_ap: v.tensor_copy(d, p),
                        reads=[ps_key], writes=[dst_key], tag=f"{tagb}e{m}")
                    continue
                scol = E.scal[:, 0:1] if kind == "q" else E.scal[:, 1:2]
                skey = "scal_lo" if kind == "q" else "scal_hi"
                sq = E.wk[6][:, 1024:1024 + blkw]
                act(P, sq, ps_ap, AF.Square, reads=[ps_key], writes=["wk6"],
                    tag=f"{tagb}q{m}")
                mm(P, E.psB[0:2, 512:512 + blkw], [(E.half_bf[:], sq)],
                   True, True, reads=["wk6", "half_bf"], writes=["psB1"],
                   tag=f"{tagb}n{m}")
                nr = E.sf[1][0:2, 0:blkw]
                dve(P, lambda v, a=nr: v.tensor_copy(a, E.psB[0:2, 512:512 + blkw]),
                    reads=["psB1"], writes=["sf1"], tag=f"{tagb}c{m}")
                dve(P, lambda v, a=nr: v.tensor_single_scalar(a, a, 1e-24,
                                                              ALU.max),
                    reads=["sf1"], writes=["sf1"], tag=f"{tagb}x{m}")
                act(P, nr, nr, AF.Ln, reads=["sf1"], writes=["sf1"],
                    tag=f"{tagb}l{m}")
                act(P, nr, nr, AF.Exp, scale=E.misc[0:2, 1:2],
                    reads=["sf1", "misc"], writes=["sf1"], tag=f"{tagb}E{m}")
                mm(P, E.psB[:, 512:512 + blkw], [(E.halfsel_f[0:2, :], nr)],
                   True, True, reads=["sf1", "halfsel_f"], writes=["psB1"],
                   tag=f"{tagb}b{m}")
                qsb = E.wk[6][:, 0:blkw]
                dve(P, lambda v, p=ps_ap, a=qsb: v.tensor_copy(a, p),
                    reads=[ps_key], writes=["wk6"], tag=f"{tagb}v{m}")
                qh = E.wk[6][:, 2048 + 0:2048 + blkw] if False else E.wk[6][:, 0:blkw]
                dve(P, lambda v, a=qsb, s=scol: v.scalar_tensor_tensor(
                    E.sf[0][:, 0:blkw], a, s, E.psB[:, 512:512 + blkw],
                    ALU.mult, ALU.mult),
                    reads=["wk6", skey, "psB1"], writes=["sf0"],
                    tag=f"{tagb}h{m}")
                qhb = E.wk[6][:, 1024:1024 + blkw]
                dve(P, lambda v, a=qhb: v.tensor_copy(a, E.sf[0][:, 0:blkw]),
                    reads=["sf0"], writes=["wk6"], tag=f"{tagb}hb{m}")
                ps2, ps2k = pB(E, 3)
                ps2 = ps2[:, 0:blkw]
                mm(P, ps2, [(E.protT[:], qhb)], True, True,
                   reads=["wk6", "protT"], writes=[ps2k], tag=f"{tagb}r{m}")
                t1 = E.sf[1][:, 0:blkw]
                dve(P, lambda v, a=t1, b=qhb, c=rot_in: v.tensor_mul(
                    a, b, c[:, 0:blkw]),
                    reads=["wk6", rot_key], writes=["sf1"], tag=f"{tagb}t1{m}")
                dve(P, lambda v, d=dpair, p=ps2, c=rot_in, a=t1:
                    v.tensor_mul(E.sf[0][:, 0:blkw], p, c[:, blkw:2 * blkw]),
                    reads=[ps2k, rot_key], writes=["sf0"], tag=f"{tagb}t2{m}")
                dve(P, lambda v, d=dpair, a=t1: v.tensor_add(
                    d, a, E.sf[0][:, 0:blkw]),
                    reads=["sf1", "sf0"], writes=[dst_key], tag=f"{tagb}t3{m}")
        P.dma("qg", E.wb[0][:, 0:8192],
              wslice_ap(ins["l1f_wqkv"], 8, 3072, 0, 1024,
                        base=li * 8 * 3072, row_stride=6 * 8 * 3072),
              writes=["wb0"], tag=f"f{li}w0")
        P.dma("qg", E.wb[1][:, 0:8192],
              wslice_ap(ins["l1f_wqkv"], 8, 3072, 1024, 1024,
                        base=li * 8 * 3072, row_stride=6 * 8 * 3072),
              writes=["wb1"], tag=f"f{li}w1")
        P.dma("qg", E.wbq[:, 0:8192],
              wslice_ap(ins["l1f_wqkv"], 8, 3072, 2048, 1024,
                        base=li * 8 * 3072, row_stride=6 * 8 * 3072),
              writes=["wbq"], tag=f"f{li}w2")
        P.dma("qs", E.rotb[:, 0:512], ins_aps["c_rotpq"], writes=["rotb"],
              tag=f"f{li}rp")
        P.dma("qs", E.rotb[:, 512:1024], ins_aps["c_rotpk"], writes=["rotb"],
              tag=f"f{li}rp2")
        spx_chs = [spx[:, c * 256:(c + 1) * 256] for c in range(NCH)]
        qkv_block(spx_chs, ["wk5"], 0, 256, E.rotb[:, 512:1024], "rotb",
                  E.wk[2], "wk2", "k", f"f{li}pk")
        qkv_block(spx_chs, ["wk5"], 0, 256, None, "rotb",
                  E.wk[7], "wk7", "v", f"f{li}pv")
        psT = E.psA[:, 0:512].bitcast(BF16)
        for h in range(16):
            pr, hhh = divmod(h, 2)
            bb = hhh * 64
            transpose(P, psT[:, (h % 8) * 64:(h % 8) * 64 + 64],
                      E.wk[7][bb:bb + 64, pr * 256:pr * 256 + 128],
                      E.idn_bf[bb:bb + 64, bb:bb + 64],
                      reads=["wk7", "idn_bf"], writes=["psA0"],
                      tag=f"f{li}pt{h}a")
            dve(P, lambda v, h=h: v.tensor_copy(
                E.wk[4][:, h * 128:h * 128 + 64],
                psT[:, (h % 8) * 64:(h % 8) * 64 + 64]),
                reads=["psA0"], writes=["wk4"], tag=f"f{li}pe{h}a")
            transpose(P, psT[:, (h % 8) * 64:(h % 8) * 64 + 64],
                      E.wk[7][bb:bb + 64, pr * 256 + 128:pr * 256 + 256],
                      E.idn_bf[bb:bb + 64, bb:bb + 64],
                      reads=["wk7", "idn_bf"], writes=["psA0"],
                      tag=f"f{li}pt{h}b")
            dve(P, lambda v, h=h: v.tensor_copy(
                E.wk[4][:, h * 128 + 64:h * 128 + 128],
                psT[:, (h % 8) * 64:(h % 8) * 64 + 64]),
                reads=["psA0"], writes=["wk4"], tag=f"f{li}pe{h}b")
        for b in range(8):
            csl = slice(b * 256, (b + 1) * 256)
            P.dma("qs", E.rotb[:, 1024:1536],
                  ins_aps["c_rotq"][:, b * 512:(b + 1) * 512],
                  writes=["rotb"], tag=f"f{li}rq{b}")
            P.dma("qs", E.rotb[:, 1536:2048],
                  ins_aps["c_rotk"][:, b * 512:(b + 1) * 512],
                  writes=["rotb"], tag=f"f{li}rk{b}")
            sx_chs = [E.xn[c][:, csl] for c in range(NCH)]
            qkv_block(sx_chs, xnk, 0, 256, E.rotb[:, 1024:1536], "rotb",
                      E.wk[0], "wk0", "q", f"f{li}b{b}q")
            qkv_block(sx_chs, xnk, 0, 256, E.rotb[:, 1536:2048], "rotb",
                      E.wk[1], "wk1", "k", f"f{li}b{b}k")
            qkv_block(sx_chs, xnk, 0, 256, None, "rotb",
                      E.wk[7], "wk7", "v", f"f{li}b{b}v")
            for h in range(16):
                pr, hhh = divmod(h, 2)
                bb = hhh * 64
                for sub in range(2):
                    transpose(P, psT[:, (h % 8) * 64:(h % 8) * 64 + 64],
                              E.wk[7][bb:bb + 64,
                                      pr * 256 + sub * 128:pr * 256 + sub * 128 + 128],
                              E.idn_bf[bb:bb + 64, bb:bb + 64],
                              reads=["wk7", "idn_bf"], writes=["psA0"],
                              tag=f"f{li}vt{b}_{h}_{sub}")
                    dve(P, lambda v, h=h, sub=sub: v.tensor_copy(
                        E.wk[3][:, h * 128 + sub * 64:h * 128 + sub * 64 + 64],
                        psT[:, (h % 8) * 64:(h % 8) * 64 + 64]),
                        reads=["psA0"], writes=["wk3"],
                        tag=f"f{li}ve{b}_{h}_{sub}")
            for h in range(16):
                pr, hhh = divmod(h, 2)
                bb = hhh * 64
                for sub in range(2):
                    qs_ap = E.wk[0][bb:bb + 64,
                                    pr * 256 + sub * 128:pr * 256 + sub * 128 + 128]
                    ps_ap, ps_key = pA(E, 2 + sub)
                    ps_ap = ps_ap[:, 0:512]
                    mm_multi(P, [
                        (ps_ap[:, 0:256], qs_ap,
                         E.wk[2][bb:bb + 64, pr * 256:(pr + 1) * 256], True, True,
                         None),
                        (ps_ap[:, 256:512], qs_ap,
                         E.wk[1][bb:bb + 64, pr * 256:(pr + 1) * 256], True, True,
                         None),
                    ], reads=["wk0", "wk1", "wk2"], writes=[ps_key],
                        tag=f"f{li}S{b}_{h}_{sub}")
                    msrc = E.mask0 if b == 0 else E.maskn
                    mkey = "mask0" if b == 0 else "maskn"
                    pq = E.sf[0][:, 0:512]
                    dve(P, lambda v, p=ps_ap, ms=msrc, sub=sub:
                        v.scalar_tensor_tensor(pq, p, 1.0,
                                               ms[:, sub * 512:(sub + 1) * 512],
                                               ALU.mult, ALU.add),
                        reads=[ps_key, mkey], writes=["sf0"],
                        tag=f"f{li}M{b}_{h}_{sub}")
                    dve(P, lambda v: v.tensor_reduce(E.sf[1][:, 0:1], pq,
                                                     AX.X, ALU.max),
                        reads=["sf0"], writes=["sf1"],
                        tag=f"f{li}x{b}_{h}_{sub}")
                    dve(P, lambda v: v.tensor_scalar_mul(E.sf[1][:, 1:2],
                                                         E.sf[1][:, 0:1], -1.0),
                        reads=["sf1"], writes=["sf1"],
                        tag=f"f{li}n{b}_{h}_{sub}")
                    pb = E.wk[6][:, 0:512]
                    act(P, pb, pq, AF.Exp, bias=E.sf[1][:, 1:2],
                        accum_out=E.sf[1][:, 2:3],
                        reads=["sf0", "sf1"], writes=["wk6", "sf1"],
                        tag=f"f{li}e{b}_{h}_{sub}")
                    dve(P, lambda v: v.reciprocal(E.sf[1][:, 3:4],
                                                  E.sf[1][:, 2:3]),
                        reads=["sf1"], writes=["sf1"],
                        tag=f"f{li}r{b}_{h}_{sub}")
                    dve(P, lambda v: v.tensor_single_scalar(
                        pb, pb, E.sf[1][:, 3:4], ALU.mult),
                        reads=["wk6", "sf1"], writes=["wk6"],
                        tag=f"f{li}p{b}_{h}_{sub}")
                    for kc in range(4):
                        transpose(P, psT[:, (kc % 4) * 128:(kc % 4) * 128 + 128],
                                  pb[:, kc * 128:(kc + 1) * 128], E.idn_bf[:],
                                  reads=["wk6", "idn_bf"], writes=["psA0"],
                                  tag=f"f{li}T{b}_{h}_{sub}_{kc}")
                    dve(P, lambda v: v.tensor_copy(E.wk[6][:, 512:1024],
                                                   psT[:, 0:512]),
                        reads=["psA0"], writes=["wk6"],
                        tag=f"f{li}Te{b}_{h}_{sub}")
                    for kc in range(4):
                        vsrc = E.wk[4] if kc < 2 else E.wk[3]
                        vcol = h * 128 + (kc % 2) * 64
                        mm(P, E.psB[bb:bb + 64, 0:128],
                           [(vsrc[:, vcol:vcol + 64],
                             E.wk[6][:, 512 + kc * 128:512 + (kc + 1) * 128])],
                           start=(kc == 0), stop=(kc == 3),
                           reads=["wk3", "wk4", "wk6"], writes=["psB0"],
                           tag=f"f{li}V{b}_{h}_{sub}_{kc}",
                           accumulate=(kc > 0), tp=(0, bb))
                    dve(P, lambda v, bb=bb: v.tensor_copy(
                        E.sf[0][bb:bb + 64, 512:640],
                        E.psB[bb:bb + 64, 0:128]),
                        reads=["psB0"], writes=["sf0"],
                        tag=f"f{li}O{b}_{h}_{sub}")
                    def st_o(eng, h=h, bb=bb, b=b, sub=sub):
                        return eng.dma_start(
                            out=dram["osp"][h * 64:(h + 1) * 64,
                                            b * 256 + sub * 128:
                                            b * 256 + sub * 128 + 128],
                            in_=E.sf[0][bb:bb + 64, 512:640])
                    P.unit("qg", st_o, reads=["sf0"], writes=["osp"], inc=16,
                           tag=f"f{li}W{b}_{h}_{sub}")
            dve(P, lambda v: v.tensor_copy(E.wk[2][:, :], E.wk[1][:, :]),
                reads=["wk1"], writes=["wk2"], tag=f"f{li}rotk{b}")
            dve(P, lambda v: v.tensor_copy(E.wk[4][:, :], E.wk[3][:, :]),
                reads=["wk3"], writes=["wk4"], tag=f"f{li}rotv{b}")
        P.dma("qg", E.wb[0][:],
              wslice_ap(ins["l1f_wo"], 8, 1024, 0, 1024,
                        base=li * 8 * 1024, row_stride=6 * 8 * 1024),
              writes=["wb0"], tag=f"f{li}wo")
        for tt in range(4):
            sl = slice(tt * 512, (tt + 1) * 512)
            for ch in range(NCH):
                half = E.wk[8] if ch < 4 else E.wk[9]
                dst = half[:, (ch % 4) * 512:(ch % 4 + 1) * 512]
                P.dma("qs", dst, dram["osp"][ch * 128:(ch + 1) * 128, sl],
                      reads=["osp"], writes=[f"wk{8 + ch // 4}"],
                      tag=f"f{li}ol{tt}_{ch}")
            for m in range(8):
                ps_ap, ps_key = (pA(E, m % 4) if m < 4 else pB(E, m % 4))
                ps_ap = ps_ap[:, 0:512]
                pairs = []
                for k in range(NCH):
                    half = E.wk[8] if k < 4 else E.wk[9]
                    pairs.append((E.wb[0][:, k * 1024 + m * 128:
                                          k * 1024 + m * 128 + 128],
                                  half[:, (k % 4) * 512:(k % 4 + 1) * 512]))
                mm(P, ps_ap, pairs, True, True,
                   reads=["wb0", "wk8", "wk9"], writes=[ps_key],
                   tag=f"f{li}o{tt}_{m}")
                xsl = E.sf[0][:, 0:512]
                P.dma("qs", xsl, ix[m * 128:(m + 1) * 128, sl],
                      reads=[ixk], writes=["sf0"], tag=f"f{li}xr{tt}_{m}")
                dve(P, lambda v, p=ps_ap: v.tensor_add(xsl, xsl, p),
                    reads=["sf0", ps_key], writes=["sf0"], tag=f"f{li}xa{tt}_{m}")
                P.dma("qs", ix2[m * 128:(m + 1) * 128, sl], xsl,
                      reads=["sf0"], writes=[ix2k], tag=f"f{li}xs{tt}_{m}")
        ix, ix2 = ix2, ix
        ixk, ix2k = ix2k, ixk
        for ch in range(NCH):
            def wtail(eng, ch=ch, ix=ix):
                return eng.dma_start(
                    out=bass.AP(dram["cc_in"].tensor, ch * 128,
                                [[1, 128], [1, 1]]),
                    in_=ix[ch * 128:(ch + 1) * 128, 2047:2048],
                    allow_slow_non_contiguous=True)
            P.unit("qs", wtail, reads=[ixk], writes=["cc_in"], inc=16,
                   tag=f"f{li}tb{ch}")
        P.unit("cc", lambda g: g.collective_compute(
            "AllGather", ALU.bypass,
            replica_groups=[[0, 1], [2, 3], [4, 5], [6, 7]],
            ins=[dram["cc_in"].opt()], outs=[dram["cc_out"].opt()]),
            reads=["cc_in"], writes=["cc_out"], inc=1, tag=f"f{li}agB")
        g_aps = [E.scal[:, 16 + c:17 + c] for c in range(NCH)]
        b_aps = [E.scal[:, 24 + c:25 + c] for c in range(NCH)]
        P.dma("qs", E.scal[:, 16:24],
              bass.AP(ins["l1f_lng"], li * 8, [[48, 128], [1, 8]]),
              writes=["scal_gb"], tag=f"f{li}lg")
        P.dma("qs", E.scal[:, 24:32],
              bass.AP(ins["l1f_lnb"], li * 8, [[48, 128], [1, 8]]),
              writes=["scal_gb"], tag=f"f{li}lb")

        def shift_loader(ch, tagl, ix=ix, ixk=ixk):
            if ch < 4:
                P.dma("qs", E.sf[0][:], ix[ch * 128:(ch + 1) * 128, :],
                      reads=[ixk], writes=["sf0"], tag=tagl)
            else:
                P.dma("qs", E.sf[0][:, 1:2048],
                      ix[ch * 128:(ch + 1) * 128, 0:2047],
                      reads=[ixk], writes=["sf0"], tag=tagl)
                P.dma("qs", E.sf[1][:, 8:9],
                      bass.AP(dram["cc_out"].tensor, ch * 128,
                              [[1, 128], [1, 1]]),
                      reads=["cc_out"], writes=["sf1"], tag=tagl + "a")
                P.dma("qs", E.sf[1][:, 9:10],
                      bass.AP(dram["cc_out"].tensor, 65536 + ch * 128,
                              [[1, 128], [1, 1]]),
                      reads=["cc_out"], writes=["sf1"], tag=tagl + "b")
                dve(P, lambda v: v.tensor_single_scalar(
                    E.sf[1][:, 8:9], E.sf[1][:, 8:9], E.selp[:, 0:1], ALU.mult),
                    reads=["sf1", "selp"], writes=["sf1"], tag=tagl + "c")
                dve(P, lambda v: v.scalar_tensor_tensor(
                    E.sf[0][:, 0:1], E.sf[1][:, 9:10], E.selp[:, 1:2],
                    E.sf[1][:, 8:9], ALU.mult, ALU.add),
                    reads=["sf1", "selp"], writes=["sf0"], tag=tagl + "d")
        emit_ln(P, E, None, None, E.xn, xnk, rowbase=0,
                gb=(g_aps, b_aps, ["scal_gb"]), x_loader=shift_loader,
                tag=f"f{li}ln")
        P.dma("qs", E.scal[:, 48:92],
              bass.AP(ins["l1f_b1p"], li * 44, [[264, 128], [1, 44]]),
              writes=["scal_b1"], tag=f"f{li}b1")
        P.dma("qs", E.svals[:, 0:8],
              bass.AP(ins["l1f_b2"], li * 8, [[48, 128], [1, 8]]),
              writes=["svals"], tag=f"f{li}b2")
        for tq in range(4):
            tsl = slice(tq * 512, (tq + 1) * 512)
            for mg in range(11):
                dstw = E.wb[0][:, 0:8 * 512].rearrange("p (k j) -> p k j", k=8)
                P.dma("qg", dstw,
                      wslice_ap(ins["l1f_w1p"], 8, 5632, mg * 512, 512,
                                base=li * 8 * 5632, row_stride=6 * 8 * 5632),
                      writes=["wb0"], tag=f"f{li}fw{tq}_{mg}")
                for mi in range(4):
                    m = mg * 4 + mi
                    ps_ap, ps_key = pA(E, mi % 2)
                    ps_ap = ps_ap[:, 0:512]
                    pairs = [(E.wb[0][:, k * 512 + mi * 128:
                                      k * 512 + mi * 128 + 128],
                              E.xn[k][:, tsl]) for k in range(NCH)]
                    mm(P, ps_ap, pairs, True, True, reads=["wb0"] + xnk,
                       writes=[ps_key], tag=f"f{li}h{tq}_{m}")
                    dst = E.wk[m // 4][:, (m % 4) * 512:(m % 4 + 1) * 512]
                    dve(P, lambda v, d=dst, p=ps_ap, m=m: v.tensor_scalar_add(
                        d, p, E.scal[:, 48 + m:49 + m]),
                        reads=[ps_key, "scal_b1"], writes=[f"wk{m // 4}"],
                        tag=f"f{li}he{tq}_{m}")
            for k in range(22):
                a_t = E.wk[k // 4][:, (k % 4) * 512:(k % 4 + 1) * 512]
                g_t = E.wk[(k + 22) // 4][:, ((k + 22) % 4) * 512:
                                          ((k + 22) % 4 + 1) * 512]
                gg = E.sf[0][:, 0:512]
                act(P, gg, g_t, AF.Gelu,
                    reads=[f"wk{(k + 22) // 4}"], writes=["sf0"],
                    tag=f"f{li}gg{tq}_{k}")
                dve(P, lambda v, a=a_t, k=k: v.tensor_mul(a, a, gg),
                    reads=[f"wk{k // 4}", "sf0"], writes=[f"wk{k // 4}"],
                    tag=f"f{li}gp{tq}_{k}")
            for mg in range(4):
                dstw = E.wb[1][:, 0:22 * 256].rearrange("p (k j) -> p k j", k=22)
                P.dma("qg", dstw,
                      wslice_ap(ins["l1f_w2p"], 22, 1024, mg * 256, 256,
                                base=li * 22 * 1024, row_stride=6 * 22 * 1024),
                      writes=["wb1"], tag=f"f{li}w2{tq}_{mg}")
                for mi in range(2):
                    m = mg * 2 + mi
                    ps_ap, ps_key = pB(E, 2 + mi)
                    ps_ap = ps_ap[:, 0:512]
                    pairs = [(E.wb[1][:, k * 256 + mi * 128:
                                      k * 256 + mi * 128 + 128],
                              E.wk[k // 4][:, (k % 4) * 512:(k % 4 + 1) * 512])
                             for k in range(22)]
                    mm(P, ps_ap, pairs, True, True,
                       reads=["wb1"] + [f"wk{i}" for i in range(6)],
                       writes=[ps_key], tag=f"f{li}y{tq}_{m}")
                    xsl2 = E.sf[0][:, 512:1024]
                    P.dma("qs", xsl2, ix[m * 128:(m + 1) * 128, tsl],
                          reads=[ixk], writes=["sf0"], tag=f"f{li}fx{tq}_{m}")
                    dve(P, lambda v, p=ps_ap, m=m: v.tensor_scalar_add(
                        E.sf[1][:, 0:512], p, E.svals[:, m:m + 1]),
                        reads=[ps_key, "svals"], writes=["sf1"],
                        tag=f"f{li}fb{tq}_{m}")
                    dve(P, lambda v: v.tensor_add(xsl2, xsl2, E.sf[1][:, 0:512]),
                        reads=["sf0", "sf1"], writes=["sf0"],
                        tag=f"f{li}fa{tq}_{m}")
                    P.dma("qs", ix2[m * 128:(m + 1) * 128, tsl], xsl2,
                          reads=["sf0"], writes=[ix2k], tag=f"f{li}fs{tq}_{m}")
        ix, ix2 = ix2, ix
        ixk, ix2k = ix2k, ixk
    for m in range(8):
        for tq in range(4):
            tsl = slice(tq * 512, (tq + 1) * 512)
            gsl = E.sf[1][:, 0:512]
            P.dma("qs", E.sf[0][:, 0:512], ix[m * 128:(m + 1) * 128, tsl],
                  reads=[ixk], writes=["sf0"], tag=f"l1g{m}_{tq}")
            act(P, gsl, E.sf[0][:, 0:512], AF.Silu,
                reads=["sf0"], writes=["sf1"], tag=f"l1gs{m}_{tq}")
            ssl = E.sf[1][:, 512:1024]
            P.dma("qs", ssl, dram["sspill"][m * 128:(m + 1) * 128, tsl],
                  reads=["sspill"], writes=["sf1"], tag=f"l1sl{m}_{tq}")
            P.dma("qs", E.sf[0][:, 0:512],
                  dram["xio"][m * 128:(m + 1) * 128, tsl],
                  reads=[dram["xio_key"]], writes=["sf0"], tag=f"l1xl{m}_{tq}")
            dve(P, lambda v, g=gsl, s_=ssl: v.tensor_mul(g, s_, g),
                reads=["sf1"], writes=["sf1"], tag=f"l1sg{m}_{tq}")
            dve(P, lambda v, g=gsl: v.tensor_add(E.sf[0][:, 0:512],
                                                 E.sf[0][:, 0:512], g),
                reads=["sf0", "sf1"], writes=["sf0"], tag=f"l1xa{m}_{tq}")
            P.dma("qs", dram["xio2"][m * 128:(m + 1) * 128, tsl],
                  E.sf[0][:, 0:512], reads=["sf0"],
                  writes=[dram["xio2_key"]], tag=f"l1xs{m}_{tq}")


def emit_layer1(P, E, ins, ins_aps, dram):
    emit_ln(P, E, dram["xio"], dram["xio_key"], E.xn,
            [f"xn{c}" for c in range(NCH)], rowbase=0, tag="l1ln")
    emit_l1_attn(P, E, ins, ins_aps, dram)
    emit_l1_ffn_residual(P, E, ins, ins_aps, dram)


def host_l1f_weights(I, half):
    import math
    w = {}
    R = np.zeros((64, 64), np.float32)
    R[0:32, 32:64] = -np.eye(32)
    R[32:64, 0:32] = np.eye(32)
    RT = R.T
    P2 = np.zeros((128, 128), np.float32)
    P2[0:64, 0:64] = RT
    P2[64:128, 64:128] = RT
    w["c_protT"] = P2
    NEG = -1e30
    def win_mask(first_block):
        m = np.zeros((256, 512), np.float32)
        il = np.arange(256)[:, None]
        jl = np.arange(512)[None, :]
        allowed = (jl >= il) & (jl <= il + 256)
        m[~allowed] = NEG
        if first_block:
            m[:, 0:256] = NEG
        return m
    def packm(m):
        return np.concatenate([m[0:128], m[128:256]], axis=1)
    w["c_mask0"] = packm(win_mask(half == 0))
    w["c_maskn"] = packm(win_mask(False))
    n = 4096
    DH, W = 64, 256
    halfi = np.arange(0, DH, 2, dtype=np.float64)
    invf = 1.0 / 10000.0 ** (halfi / DH)
    t = np.arange(n, dtype=np.float64)
    fr = t[:, None] * invf[None, :]
    fr = np.concatenate([fr, fr], -1)
    base = (halfi + 0.4 * DH) / (1.4 * DH)
    power = (t - n // 2) / (W // 2)
    sc = base[None, :] ** power[:, None]
    sc = np.concatenate([sc, sc], -1)
    cosq = (np.cos(fr) * sc).astype(np.float32)
    sinq = (np.sin(fr) * sc).astype(np.float32)
    cosk = (np.cos(fr) / sc).astype(np.float32)
    sink = (np.sin(fr) / sc).astype(np.float32)
    def packrot(cos_, sin_, lo, hi):
        nb = (hi - lo) // 256
        out = np.zeros((128, nb * 512), np.float32)
        for b2 in range(nb):
            seg = slice(lo + b2 * 256, lo + (b2 + 1) * 256)
            c = cos_[seg].T
            s = sin_[seg].T
            out[0:64, b2 * 512:b2 * 512 + 256] = c
            out[64:128, b2 * 512:b2 * 512 + 256] = c
            out[0:64, b2 * 512 + 256:(b2 + 1) * 512] = s
            out[64:128, b2 * 512 + 256:(b2 + 1) * 512] = s
        return out
    lo = half * 2048
    w["c_rotq"] = packrot(cosq, sinq, lo, lo + 2048)
    w["c_rotk"] = packrot(cosk, sink, lo, lo + 2048)
    w["c_rotpq"] = packrot(cosq, sinq, 1792, 2048)
    w["c_rotpk"] = packrot(cosk, sink, 1792, 2048)
    w["l1f_wqkv"] = np.concatenate(
        [pack_lhsT(I["L1_ffn_Wqkv"][i]) for i in range(6)], axis=1)
    w["l1f_wo"] = np.concatenate(
        [pack_lhsT(I["L1_ffn_Wo"][i]) for i in range(6)], axis=1)
    W1 = I["L1_ffn_W1"]
    W2 = I["L1_ffn_W2"]
    b1 = I["L1_ffn_b1"]
    w1p, w2p, b1p = [], [], []
    for i in range(6):
        a = W1[i][:, 0:2730]
        g = W1[i][:, 2730:5460]
        ap_ = np.zeros((1024, HP), np.float32)
        gp = np.zeros((1024, HP), np.float32)
        ap_[:, 0:2730] = a
        gp[:, 0:2730] = g
        w1p.append(pack_lhsT(np.concatenate([ap_, gp], axis=1)))
        w2pad = np.zeros((HP, 1024), np.float32)
        w2pad[0:2730] = W2[i]
        w2p.append(pack_lhsT(w2pad))
        bp = np.zeros(2 * HP, np.float32)
        bp[0:2730] = b1[i][0:2730]
        bp[HP:HP + 2730] = b1[i][2730:5460]
        b1p.append(pack_bias(bp))
    w["l1f_w1p"] = np.concatenate(w1p, axis=1)
    w["l1f_w2p"] = np.concatenate(w2p, axis=1)
    w["l1f_b1p"] = np.concatenate(b1p, axis=1)
    qs = np.zeros((8, 128), np.float32)
    ks = np.zeros((8, 128), np.float32)
    for i in range(6):
        qs[i, 0:64] = I["L1_ffn_q_scale"][i]
        qs[i, 64:128] = I["L1_ffn_q_scale"][i]
        ks[i, 0:64] = I["L1_ffn_k_scale"][i]
        ks[i, 64:128] = I["L1_ffn_k_scale"][i]
    w["l1f_qs"] = qs
    w["l1f_ks"] = ks
    lng = np.zeros((128, 48), np.float32)
    lnb = np.zeros((128, 48), np.float32)
    b2a = np.zeros((128, 48), np.float32)
    for i in range(6):
        lng[:, i * 8:(i + 1) * 8] = pack_bias(I["L1_ffn_ln_g"][i])
        lnb[:, i * 8:(i + 1) * 8] = pack_bias(I["L1_ffn_ln_b"][i])
        b2a[:, i * 8:(i + 1) * 8] = pack_bias(I["L1_ffn_b2"][i])
    w["l1f_lng"] = lng
    w["l1f_lnb"] = lnb
    w["l1f_b2"] = b2a
    import ml_dtypes
    for k in ("c_protT", "c_mask0", "c_maskn", "c_rotq", "c_rotk",
              "c_rotpq", "c_rotpk"):
        w[k] = w[k].astype(ml_dtypes.bfloat16)
    return w
